# version 08
# speedup vs baseline: 1.0714x; 1.0169x over previous
from contextlib import ExitStack
import numpy as np
import concourse.bass as bass
import concourse.mybir as mybir
from concourse.bass_utils import run_bass_kernel_spmd

F32 = mybir.dt.float32
BF16 = mybir.dt.bfloat16
U32 = mybir.dt.uint32
ALU = mybir.AluOpType
ACT = mybir.ActivationFunctionType
AX = mybir.AxisListType.X

DEBUG_ATTN = False
P = 128
D = 1024
FC = 8
KD, VD, HEADS, DK, DV = 512, 1024, 4, 128, 256
INTER, NE = 512, 16
TCORE = 2048
W = 2049
CH = 64
NCH = TCORE // CH
NG = TCORE // P
EPS = 1e-5


class _Rec:
    def __init__(self, eng):
        object.__setattr__(self, "_e", eng)
        object.__setattr__(self, "last", None)

    def __getattr__(self, n):
        a = getattr(self._e, n)
        if callable(a):
            def w(*args, **kw):
                r = a(*args, **kw)
                if hasattr(r, "then_inc") and n not in ("nop", "wait_ge", "drain"):
                    object.__setattr__(self, "last", r)
                return r
            return w
        return a


def build(phases=("p1", "ffn1", "ln2", "attn", "ln3", "ffn2")):
    nc = bass.Bass("TRN2", target_bir_lowering=False)

    x_in = nc.dram_tensor("x", [W, D], F32, kind="ExternalInput").ap()
    out_t = nc.dram_tensor("out", [TCORE, D], F32, kind="ExternalOutput").ap()

    def din(name, shape, dt=F32):
        return nc.dram_tensor(name, list(shape), dt, kind="ExternalInput").ap()

    gate_wT = din("gate_wT", (D, NE))
    w1T = din("w1T", (NE, D, INTER))
    w3T = din("w3T", (NE, D, INTER))
    w2T = din("w2T", (NE, INTER, D))
    ws1T = din("ws1T", (D, INTER))
    ws3T = din("ws3T", (D, INTER))
    ws2T = din("ws2T", (INTER, D))
    xp_W1T = din("xp_W1T", (D, 160))
    xp_W2p = din("xp_W2p", (P, D))
    xp_W2g = din("xp_W2g", (32, D))
    r_WT = din("r_WT", (D, KD))
    k_WT = din("k_WT", (D, KD))
    v_WT = din("v_WT", (D, VD))
    w_AT = din("w_AT", (D, 64))
    w_BT = din("w_BT", (64, KD))
    g_AT = din("g_AT", (D, 64))
    g_BT = din("g_BT", (64, VD))
    o_WT = din("o_WT", (VD, D))
    xp_mu_p = din("xp_mu_p", (P, FC))
    xbias_p = din("xbias_p", (P, 5 * FC))
    wb_p = din("wb_p", (P, HEADS))
    bonus_p = din("bonus_p", (P, HEADS))
    gnw_rep = din("gnw_rep", (P, VD))
    gnb_rep = din("gnb_rep", (P, VD))
    gb_rep = din("gb_rep", (P, VD))
    gbias_rep = din("gbias_rep", (P, NE))
    ident_f = din("ident_f", (P, P))
    tri_neg = din("tri_neg", (P, P))
    neg_big = din("neg_big", (P, NE))
    rankm = din("rankm", (P, 8))
    mask_st = din("mask_st", (P, P), U32)
    onehotE = din("onehotE", (NE, NE * P))

    if DEBUG_ATTN:
        dbg_o = nc.dram_tensor("dbg_o", [P, NG * VD], F32, kind="ExternalOutput").ap()
        dbg_r = nc.dram_tensor("dbg_r", [P, HEADS * TCORE], F32, kind="ExternalOutput").ap()
        dbg_k = nc.dram_tensor("dbg_k", [P, HEADS * TCORE], F32, kind="ExternalOutput").ap()
        dbg_t2w = nc.dram_tensor("dbg_t2w", [64, TCORE], F32, kind="ExternalOutput").ap()
        dbg_t2g = nc.dram_tensor("dbg_t2g", [64, TCORE], F32, kind="ExternalOutput").ap()
        dbg_v = nc.dram_tensor("dbg_v", [FC, P, TCORE], F32, kind="ExternalOutput").ap()
        dbg_xl = nc.dram_tensor("dbg_xl", [P, TCORE], F32, kind="ExternalOutput").ap()
        dbg_qt = nc.dram_tensor("dbg_qt", [P, TCORE], F32, kind="ExternalOutput").ap()
        dbg_ktf = nc.dram_tensor("dbg_ktf", [P, 1024], F32, kind="ExternalOutput").ap()
        dbg_A = nc.dram_tensor("dbg_A", [P, DV], F32, kind="ExternalOutput").ap()
        dbg_mid = nc.dram_tensor("dbg_mid", [P, NCH], F32, kind="ExternalOutput").ap()
        dbg_cwc = nc.dram_tensor("dbg_cwc", [P, NCH], F32, kind="ExternalOutput").ap()
        dbg_logf = nc.dram_tensor("dbg_logf", [P, NCH], F32, kind="ExternalOutput").ap()
        dbg_dtk = nc.dram_tensor("dbg_dtk", [P, NG], F32, kind="ExternalOutput").ap()
        dbg_ktok = nc.dram_tensor("dbg_ktok", [P, 8 * P], F32, kind="ExternalOutput").ap()
        dbg_vtok = nc.dram_tensor("dbg_vtok", [P, 8 * DV], F32, kind="ExternalOutput").ap()
    cc_in = nc.dram_tensor("cc_in", [HEADS, P, DV + 1], F32)
    cc_out = nc.dram_tensor("cc_out", [HEADS, 4, P, DV + 1], F32)
    v_spill = nc.dram_tensor("v_spill", [FC, P, TCORE], BF16)

    ctx = ExitStack()
    with ctx:
        E = ctx.enter_context
        semnames = ["dx", "dxa", "dxb", "dw", "dwA", "dwB", "dw2A", "dw2B", "daA", "daB", "da", "mm", "vx", "sx", "gx", "cc", "st"]
        sems = {n: E(nc.semaphore(n)) for n in semnames}
        cnt = {n: 0 for n in semnames}

        def inc(name, k=1):
            cnt[name] += k
            return sems[name]

        def wt(eng, name, val=None):
            eng.wait_ge(sems[name], cnt[name] if val is None else val)

        def fire(eng, name):
            eng.drain()
            eng.last.then_inc(inc(name), 1)

        def dmago(eng, name, out, in_, **kw):
            eng.dma_start(out=out, in_=in_, **kw).then_inc(inc(name, 16), 16)

        xres = E(nc.sbuf_tensor("xres", [P, FC, W], F32))
        sb_idf = E(nc.sbuf_tensor("sb_idf", [P, P], F32))
        sb_idb = E(nc.sbuf_tensor("sb_idb", [P, P], BF16))
        sb_tri = E(nc.sbuf_tensor("sb_tri", [P, P], F32))
        sb_mask = E(nc.sbuf_tensor("sb_mask", [P, P], U32))
        sb_zero = E(nc.sbuf_tensor("sb_zero", [P, P], BF16))
        sb_ones1 = E(nc.sbuf_tensor("sb_ones1", [1, P], F32))
        sb_onescol = E(nc.sbuf_tensor("sb_onescol", [P, 1], F32))
        sb_gatew = E(nc.sbuf_tensor("sb_gatew", [P, FC, NE], BF16))
        sb_gbias = E(nc.sbuf_tensor("sb_gbias", [P, NE], F32))
        sb_negbig = E(nc.sbuf_tensor("sb_negbig", [P, NE], F32))
        sb_xpmu = E(nc.sbuf_tensor("sb_xpmu", [P, FC], F32))
        sb_xbias = E(nc.sbuf_tensor("sb_xbias", [P, 5 * FC], F32))
        sb_wb = E(nc.sbuf_tensor("sb_wb", [P, HEADS], F32))
        sb_u = E(nc.sbuf_tensor("sb_u", [P, HEADS], F32))
        sb_rank = E(nc.sbuf_tensor("sb_rank", [P, 8], F32))
        sb_eps = E(nc.sbuf_tensor("sb_eps", [P, 1], F32))
        sb_nh = E(nc.sbuf_tensor("sb_nh", [P, 1], F32))
        sb_half = E(nc.sbuf_tensor("sb_half", [P, 1], F32))

        block = E(nc.Block())

        def _wrap(mk):
            def runner(f):
                mk(lambda eng: f(_Rec(eng)))
            return runner
        TE = _wrap(block.tensor)
        VE = _wrap(block.vector)
        SE = _wrap(block.scalar)
        GE = _wrap(block.gpsimd)
        SY = _wrap(block.sync)

        def _(s):
            s.nop()
        SE(_)

        def _(sy):
            for dst, src in [(sb_idf, ident_f), (sb_tri, tri_neg), (sb_mask, mask_st),
                             (sb_gbias, gbias_rep), (sb_negbig, neg_big),
                             (sb_xpmu, xp_mu_p), (sb_xbias, xbias_p), (sb_wb, wb_p),
                             (sb_u, bonus_p), (sb_rank, rankm)]:
                dmago(sy, "dx", dst[:, :], src[:, :])
        SY(_)

        def _(g):
            dmago(g, "dw", sb_idb[:, :], ident_f[:, :])
            dmago(g, "dw", sb_gatew.ap()[:, :, :],
                  gate_wT.rearrange("(a p) b -> p a b", p=P))
        GE(_)

        def _(v):
            v.memset(sb_ones1[:, :], 1.0)
            v.memset(sb_onescol[:, :], 1.0)
            v.memset(sb_zero[:, :], 0.0)
            v.memset(sb_eps[:, :], EPS)
            v.memset(sb_nh[:, :], -0.5)
            v.memset(sb_half[:, :], 0.5)
            fire(v, "vx")
        VE(_)

        def join_all():
            for mk in (TE, VE, SE, GE, SY):
                def f(eng):
                    for s in semnames:
                        if s != "st":
                            wt(eng, s)
                mk(f)

        def emit_load_ln1(xln):
            with ExitStack() as stk1:
                tok0 = stk1.enter_context(nc.sbuf_tensor("p1_tok0", [P, D], F32))
                tok1 = stk1.enter_context(nc.sbuf_tensor("p1_tok1", [P, D], F32))
                sq_t = stk1.enter_context(nc.sbuf_tensor("p1_sq", [P, D], F32))
                stat = stk1.enter_context(nc.sbuf_tensor("p1_stat", [P, 6], F32))
                ln_t = stk1.enter_context(nc.sbuf_tensor("p1_ln", [P, D], BF16))
                tp0 = stk1.enter_context(nc.psum_tensor("p1_ps0", [P, 512], F32))
                tp1 = stk1.enter_context(nc.psum_tensor("p1_ps1", [P, 1024], BF16))

                toks = [tok0, tok1]
                dx0 = cnt["dx"]
                vx_done = {}

                for tt in range(17):
                    def f(sy, tt=tt):
                        if tt >= 2:
                            wt(sy, "vx", vx_done[tt - 2])
                        par = "dxa" if tt % 2 == 0 else "dxb"
                        if tt < 16:
                            dmago(sy, par, toks[tt % 2][:, :],
                                  x_in[1 + tt * P: 1 + (tt + 1) * P, :])
                        else:
                            dmago(sy, "dxa", toks[0][0:1, :], x_in[0:1, :])
                    SY(f)

                    halo = tt == 16
                    rows = 1 if halo else P
                    buf = toks[0] if halo else toks[tt % 2]
                    col0 = 0 if halo else 1 + tt * P

                    def f(s, tt=tt, buf=buf, rows=rows):
                        par = "dxa" if tt % 2 == 0 else "dxb"
                        s.wait_ge(sems[par], 16 * (tt // 2 + 1))
                        s.activation(out=sq_t[0:rows, :], in_=buf[0:rows, :],
                                     func=ACT.Square)
                        fire(s, "sx")
                    SE(f)

                    def f(v, tt=tt, buf=buf, rows=rows):
                        par = "dxa" if tt % 2 == 0 else "dxb"
                        v.wait_ge(sems[par], 16 * (tt // 2 + 1))
                        v.tensor_reduce(out=stat[0:rows, 0:1], in_=buf[0:rows, :],
                                        axis=AX, op=ALU.add)
                        wt(v, "sx")
                        v.tensor_reduce(out=stat[0:rows, 1:2], in_=sq_t[0:rows, :],
                                        axis=AX, op=ALU.add)
                        v.drain()
                        v.tensor_scalar(out=stat[0:rows, 2:3], in0=stat[0:rows, 0:1],
                                        scalar1=1.0 / D, scalar2=None, op0=ALU.mult)
                        v.tensor_scalar(out=stat[0:rows, 3:4], in0=stat[0:rows, 1:2],
                                        scalar1=1.0 / D, scalar2=None, op0=ALU.mult)
                        v.drain()
                        v.tensor_tensor(out=stat[0:rows, 4:5], in0=stat[0:rows, 2:3],
                                        in1=stat[0:rows, 2:3], op=ALU.mult)
                        v.drain()
                        v.tensor_tensor(out=stat[0:rows, 4:5], in0=stat[0:rows, 3:4],
                                        in1=stat[0:rows, 4:5], op=ALU.subtract)
                        fire(v, "vx")
                    VE(f)

                    def f(s, rows=rows):
                        wt(s, "vx")
                        s.activation(out=stat[0:rows, 5:6], in_=stat[0:rows, 4:5],
                                     func=ACT.Ln, bias=sb_eps[0:rows, :])
                        s.drain()
                        s.activation(out=stat[0:rows, 5:6], in_=stat[0:rows, 5:6],
                                     func=ACT.Exp, scale=sb_nh[0:rows, :])
                        fire(s, "sx")
                    SE(f)

                    def f(v, buf=buf, rows=rows):
                        wt(v, "sx")
                        v.tensor_scalar(out=ln_t[0:rows, :], in0=buf[0:rows, :],
                                        scalar1=stat[0:rows, 2:3],
                                        scalar2=stat[0:rows, 5:6],
                                        op0=ALU.subtract, op1=ALU.mult)
                        fire(v, "vx")
                    VE(f)

                    for fc in range(FC):
                        def f(t, fc=fc, buf=buf, rows=rows):
                            wt(t, "vx")
                            wt(t, "dx")
                            t.transpose(tp0[:, 0:rows], buf[0:rows, fc * P:(fc + 1) * P],
                                        sb_idf[0:rows, 0:rows])
                            wt(t, "dw")
                            t.transpose(tp1[:, 0:rows], ln_t[0:rows, fc * P:(fc + 1) * P],
                                        sb_idb[0:rows, 0:rows])
                            t.last.then_inc(inc("mm"), 1)
                        TE(f)

                        def f(v, fc=fc, col0=col0, rows=rows, tt=tt,
                              last=(fc == FC - 1)):
                            wt(v, "mm")
                            v.tensor_copy(xres[:, fc, col0:col0 + rows], tp0[:, 0:rows])
                            v.tensor_copy(xln[:, fc, col0:col0 + rows], tp1[:, 0:rows])
                            fire(v, "vx")
                            if last:
                                vx_done[tt] = cnt["vx"]
                        VE(f)

                        def f(t):
                            wt(t, "vx")
                        TE(f)

            join_all()

        def emit_ln(xln, c0, c1, tag=""):
            with ExitStack() as stk2:
                sqb0 = stk2.enter_context(nc.sbuf_tensor(tag + "ln_sq0", [P, 512], F32))
                sqb1 = stk2.enter_context(nc.sbuf_tensor(tag + "ln_sq1", [P, 512], F32))
                sums = stk2.enter_context(nc.sbuf_tensor(tag + "ln_sums", [1, W], F32))
                sqs = stk2.enter_context(nc.sbuf_tensor(tag + "ln_sqs", [1, W], F32))
                s_str = stk2.enter_context(nc.sbuf_tensor(tag + "ln_s", [1, W], F32))
                b_str = stk2.enter_context(nc.sbuf_tensor(tag + "ln_b", [1, W], F32))
                srep = stk2.enter_context(nc.sbuf_tensor(tag + "ln_srep", [P, W], F32))
                brep = stk2.enter_context(nc.sbuf_tensor(tag + "ln_brep", [P, W], F32))
                psA = stk2.enter_context(nc.psum_tensor(tag + "ln_pA", [P, 512], F32))
                psB = stk2.enter_context(nc.psum_tensor(tag + "ln_pB", [P, 512], F32))
                psR = stk2.enter_context(nc.psum_tensor(tag + "ln_pR", [P, 512], F32))

                sqb = [sqb0, sqb1]
                slabs = []
                c = c0
                while c < c1:
                    wdt = min(512, c1 - c)
                    slabs.append((c, wdt))
                    c += wdt

                sx_sq = {}
                for (c, wdt) in slabs:
                    for fc in range(FC):
                        def f(s, c=c, wdt=wdt, fc=fc):
                            if fc >= 2 or c != slabs[0][0]:
                                wt(s, "mm")
                            s.activation(out=sqb[fc % 2][:, 0:wdt],
                                         in_=xres[:, fc, c:c + wdt], func=ACT.Square)
                            fire(s, "sx")
                            sx_sq[(c, fc)] = cnt["sx"]
                        SE(f)

                        def f(t, c=c, wdt=wdt, fc=fc):
                            wt(t, "sx", sx_sq[(c, fc)])
                            t.matmul(psA[0:1, 0:wdt], sb_onescol[:, :],
                                     xres[:, fc, c:c + wdt],
                                     start=(fc == 0), stop=(fc == FC - 1))
                            t.matmul(psB[0:1, 0:wdt], sb_onescol[:, :],
                                     sqb[fc % 2][:, 0:wdt],
                                     start=(fc == 0), stop=(fc == FC - 1))
                            t.last.then_inc(inc("mm"), 1)
                        TE(f)

                    def f(v, c=c, wdt=wdt):
                        wt(v, "mm")
                        v.tensor_copy(sums[0:1, c:c + wdt], psA[0:1, 0:wdt])
                        v.tensor_copy(sqs[0:1, c:c + wdt], psB[0:1, 0:wdt])
                        fire(v, "vx")
                    VE(f)

                    def f(t):
                        wt(t, "vx")
                    TE(f)

                def f(v):
                    wt(v, "mm")
                    v.tensor_scalar(out=sums[0:1, c0:c1], in0=sums[0:1, c0:c1],
                                    scalar1=1.0 / D, scalar2=None, op0=ALU.mult)
                    v.tensor_scalar(out=sqs[0:1, c0:c1], in0=sqs[0:1, c0:c1],
                                    scalar1=1.0 / D, scalar2=None, op0=ALU.mult)
                    v.drain()
                    v.tensor_tensor(out=s_str[0:1, c0:c1], in0=sums[0:1, c0:c1],
                                    in1=sums[0:1, c0:c1], op=ALU.mult)
                    v.drain()
                    v.tensor_tensor(out=s_str[0:1, c0:c1], in0=sqs[0:1, c0:c1],
                                    in1=s_str[0:1, c0:c1], op=ALU.subtract)
                    fire(v, "vx")
                VE(f)

                def f(s):
                    wt(s, "vx")
                    s.activation(out=s_str[0:1, c0:c1], in_=s_str[0:1, c0:c1],
                                 func=ACT.Ln, bias=sb_eps[0:1, :])
                    s.drain()
                    s.activation(out=s_str[0:1, c0:c1], in_=s_str[0:1, c0:c1],
                                 func=ACT.Exp, scale=sb_nh[0:1, :])
                    fire(s, "sx")
                SE(f)

                def f(v):
                    wt(v, "sx")
                    v.scalar_tensor_tensor(out=b_str[0:1, c0:c1], in0=sums[0:1, c0:c1],
                                           scalar=-1.0, in1=s_str[0:1, c0:c1],
                                           op0=ALU.mult, op1=ALU.mult)
                    fire(v, "vx")
                VE(f)

                for (strip, rep) in [(s_str, srep), (b_str, brep)]:
                    for (c, wdt) in slabs:
                        def f(t, strip=strip, c=c, wdt=wdt):
                            wt(t, "vx")
                            t.matmul(psR[:, 0:wdt], sb_ones1[:, :],
                                     strip[0:1, c:c + wdt], start=True, stop=True)
                            t.last.then_inc(inc("mm"), 1)
                        TE(f)

                        def f(v, rep=rep, c=c, wdt=wdt):
                            wt(v, "mm")
                            v.tensor_copy(rep[:, c:c + wdt], psR[:, 0:wdt])
                            fire(v, "vx")
                        VE(f)

                        def f(t):
                            wt(t, "vx")
                        TE(f)

                def f(v):
                    for fc in range(FC):
                        v.scalar_tensor_tensor(out=xln[:, fc, c0:c1],
                                               in0=xres[:, fc, c0:c1], scalar=1.0,
                                               in1=srep[:, c0:c1],
                                               op0=ALU.mult, op1=ALU.mult)
                        v.drain()
                        v.tensor_tensor(out=xln[:, fc, c0:c1], in0=xln[:, fc, c0:c1],
                                        in1=brep[:, c0:c1], op=ALU.add)
                        v.drain()
                    fire(v, "vx")
                VE(f)

            join_all()

        def emit_ffn(xln, tag, c0, c1):
            ntt = (c1 + P - 1) // P
            tcms = []
            c = c0
            while c < c1:
                rem = c1 - c
                wdt = min(512, rem)
                if rem == 513:
                    wdt = 256
                tcms.append((c, wdt))
                c += wdt

            with ExitStack() as stk3:
                scores = stk3.enter_context(nc.sbuf_tensor(tag + "sc", [P, ntt, NE], F32))
                rt0 = stk3.enter_context(nc.sbuf_tensor(tag + "r0", [P, ntt, NE], F32))
                rt1 = stk3.enter_context(nc.sbuf_tensor(tag + "r1", [P, ntt, NE], F32))
                rtg = stk3.enter_context(nc.sbuf_tensor(tag + "rg", [P, ntt, 4], F32))
                rtg2 = stk3.enter_context(nc.sbuf_tensor(tag + "rg2", [P, ntt, 4], F32))
                rtm = stk3.enter_context(nc.sbuf_tensor(tag + "rm", [P, ntt], F32))
                keepg = stk3.enter_context(nc.sbuf_tensor(tag + "kg", [P, ntt, 4], U32))
                comb = stk3.enter_context(nc.sbuf_tensor(tag + "cbm", [P, ntt, NE], F32))
                combE = stk3.enter_context(nc.sbuf_tensor(tag + "cbE", [NE, ntt * P], F32))
                oneh = stk3.enter_context(nc.sbuf_tensor(tag + "oneh", [NE, NE * P], F32))
                crep = stk3.enter_context(nc.sbuf_tensor(tag + "crep", [P, W], BF16))
                w1a = stk3.enter_context(nc.sbuf_tensor(tag + "w1a", [P, FC, INTER], BF16))
                w1b = stk3.enter_context(nc.sbuf_tensor(tag + "w1b", [P, FC, INTER], BF16))
                w3a = stk3.enter_context(nc.sbuf_tensor(tag + "w3a", [P, FC, INTER], BF16))
                w3b = stk3.enter_context(nc.sbuf_tensor(tag + "w3b", [P, FC, INTER], BF16))
                w2a = stk3.enter_context(nc.sbuf_tensor(tag + "w2a", [P, 4, D], BF16))
                w2b = stk3.enter_context(nc.sbuf_tensor(tag + "w2b", [P, 4, D], BF16))
                hp = stk3.enter_context(nc.sbuf_tensor(tag + "hp", [P, 4, 512], BF16))
                h3t0 = stk3.enter_context(nc.sbuf_tensor(tag + "h3t0", [P, 512], BF16))
                h3t1 = stk3.enter_context(nc.sbuf_tensor(tag + "h3t1", [P, 512], BF16))
                yt0 = stk3.enter_context(nc.sbuf_tensor(tag + "yt0", [P, 4, 256], BF16))
                yt1 = stk3.enter_context(nc.sbuf_tensor(tag + "yt1", [P, 4, 256], BF16))
                p1a = stk3.enter_context(nc.psum_tensor(tag + "p1a", [P, 512], F32))
                p1b = stk3.enter_context(nc.psum_tensor(tag + "p1b", [P, 512], F32))
                p3a = stk3.enter_context(nc.psum_tensor(tag + "p3a", [P, 512], F32))
                p3b = stk3.enter_context(nc.psum_tensor(tag + "p3b", [P, 512], F32))
                py0 = stk3.enter_context(nc.psum_tensor(tag + "py0", [P, 4, 256], F32))
                py1 = stk3.enter_context(nc.psum_tensor(tag + "py1", [P, 4, 256], F32))

                ph1 = [p1a, p1b]
                ph3 = [p3a, p3b]
                pys = [py0, py1]
                w1bufs, w3bufs, w2bufs = [w1a, w1b], [w3a, w3b], [w2a, w2b]
                h3ts = [h3t0, h3t1]
                yts = [yt0, yt1]

                def f(sy):
                    dmago(sy, "dx", oneh[:, :], onehotE[:, :])
                SY(f)

                def f(v):
                    v.memset(scores.ap().rearrange("p a e -> p (a e)"), 0.0)
                    if c0 > 0:
                        v.memset(xln[:, :, 0:c0].rearrange("p a b -> p (a b)"), 0.0)
                    fire(v, "vx")
                VE(f)

                rt_vx0 = cnt["vx"]
                sx_rt = {}
                for tt in range(ntt):
                    t0 = tt * P
                    rows = min(P, c1 - t0)

                    def f(t, tt=tt, t0=t0, rows=rows):
                        wt(t, "vx", rt_vx0)
                        if tt >= 1:
                            wt(t, "sx", sx_rt[tt - 1])
                        for fc in range(FC):
                            t.matmul(p1a[0:rows, 0:NE], xln[:, fc, t0:t0 + rows],
                                     sb_gatew[:, fc, :],
                                     start=(fc == 0), stop=(fc == FC - 1))
                        t.last.then_inc(inc("mm"), 1)
                    TE(f)

                    def f(s, tt=tt, rows=rows):
                        wt(s, "mm")
                        wt(s, "vx", rt_vx0)
                        s.activation(out=scores[0:rows, tt, :], in_=p1a[0:rows, 0:NE],
                                     func=ACT.Sigmoid)
                        fire(s, "sx")
                        sx_rt[tt] = cnt["sx"]
                    SE(f)

                def f(v):
                    wt(v, "sx")
                    v.tensor_tensor(out=rt0[:, :, :], in0=scores.ap(),
                                    in1=sb_gbias[:, :].unsqueeze(1)
                                    .to_broadcast([P, ntt, NE]), op=ALU.add)
                    v.drain()
                    v.tensor_reduce(out=rtg.ap().rearrange("p a g -> p (a g)"),
                                    in_=rt0.ap().rearrange("p a (g e) -> p (a g) e", e=4),
                                    axis=AX, op=ALU.max)
                    v.drain()
                    v.tensor_reduce(out=rtm[:, :], in_=rtg[:, :, :], axis=AX, op=ALU.max)
                    v.drain()
                    v.tensor_tensor(out=rtg2[:, :, :], in0=rtg[:, :, :],
                                    in1=rtm.ap().unsqueeze(2).to_broadcast([P, ntt, 4]),
                                    op=ALU.is_ge)
                    v.drain()
                    v.scalar_tensor_tensor(out=rtg2[:, :, :], in0=rtg2[:, :, :],
                                           scalar=-1e30, in1=rtg[:, :, :],
                                           op0=ALU.mult, op1=ALU.add)
                    v.drain()
                    v.tensor_reduce(out=rtm[:, :], in_=rtg2[:, :, :], axis=AX, op=ALU.max)
                    v.drain()
                    v.tensor_tensor(out=keepg[:, :, :], in0=rtg[:, :, :],
                                    in1=rtm.ap().unsqueeze(2).to_broadcast([P, ntt, 4]),
                                    op=ALU.is_ge)
                    v.drain()
                    v.tensor_scalar(out=rtg2[:, :, :], in0=keepg[:, :, :],
                                    scalar1=1e30, scalar2=-1e30,
                                    op0=ALU.mult, op1=ALU.add)
                    v.drain()
                    v.tensor_tensor(out=rt1.ap().rearrange("p a (g e) -> p a g e", e=4),
                                    in0=rt0.ap().rearrange("p a (g e) -> p a g e", e=4),
                                    in1=rtg2.ap().unsqueeze(3)
                                    .to_broadcast([P, ntt, 4, 4]),
                                    op=ALU.add)
                    v.drain()
                    v.tensor_copy(rt0.ap().rearrange("p a e -> p (a e)"),
                                  rt1.ap().rearrange("p a e -> p (a e)"))
                    v.drain()
                    for _i in range(3):
                        v.tensor_reduce(out=rtm[:, :], in_=rt0[:, :, :], axis=AX,
                                        op=ALU.max)
                        v.drain()
                        v.tensor_tensor(out=comb[:, :, :], in0=rt0[:, :, :],
                                        in1=rtm.ap().unsqueeze(2)
                                        .to_broadcast([P, ntt, NE]), op=ALU.is_ge)
                        v.drain()
                        v.scalar_tensor_tensor(out=rt0[:, :, :], in0=comb[:, :, :],
                                               scalar=-1e30, in1=rt0[:, :, :],
                                               op0=ALU.mult, op1=ALU.add)
                        v.drain()
                    v.tensor_reduce(out=rtm[:, :], in_=rt0[:, :, :], axis=AX, op=ALU.max)
                    v.drain()
                    v.tensor_tensor(out=comb[:, :, :], in0=rt1[:, :, :],
                                    in1=rtm.ap().unsqueeze(2).to_broadcast([P, ntt, NE]),
                                    op=ALU.is_ge)
                    v.drain()
                    v.tensor_tensor(out=comb[:, :, :], in0=comb[:, :, :],
                                    in1=scores.ap(), op=ALU.mult)
                    fire(v, "vx")
                VE(f)

                for tt in range(ntt):
                    rows = min(P, c1 - tt * P)

                    def f(t, tt=tt, rows=rows):
                        wt(t, "vx")
                        t.transpose(p1a[0:NE, 0:rows], comb[0:rows, tt, :],
                                    sb_idf[0:rows, 0:rows])
                        t.last.then_inc(inc("mm"), 1)
                    TE(f)

                    def f(v, tt=tt, rows=rows):
                        wt(v, "mm")
                        v.tensor_copy(combE[:, tt * P:tt * P + rows], p1a[0:NE, 0:rows])
                        fire(v, "vx")
                    VE(f)

                    def f(t):
                        wt(t, "vx")
                    TE(f)

                def load_w13(g, e):
                    b = e % 2
                    sem = "dwA" if b == 0 else "dwB"
                    s1 = w1T[e] if e < NE else ws1T
                    s3 = w3T[e] if e < NE else ws3T
                    dmago(g, sem, w1bufs[b].ap()[:, :, :],
                          s1.rearrange("(a p) b -> p a b", p=P))
                    dmago(g, sem, w3bufs[b].ap()[:, :, :],
                          s3.rearrange("(a p) b -> p a b", p=P))
                    return cnt[sem]

                def load_w2(g, e):
                    b = e % 2
                    sem = "dw2A" if b == 0 else "dw2B"
                    s2 = w2T[e] if e < NE else ws2T
                    dmago(g, sem, w2bufs[b].ap()[:, :, :],
                          s2.rearrange("(a p) b -> p a b", p=P))
                    return cnt[sem]

                dw13, dw2 = {}, {}
                mm_h, mm_y = {}, {}
                ev_i = {}
                ysx = {}
                yda = {}
                crep_vx = {}

                def f(g):
                    for e0 in (0, 1):
                        dw13[e0] = load_w13(g, e0)
                        dw2[e0] = load_w2(g, e0)
                GE(f)

                NEXP = NE + 1
                for e in range(NEXP):
                    if e < NE:
                        for (c, wdt) in tcms:
                            def f(t, e=e, c=c, wdt=wdt):
                                wt(t, "vx")
                                wt(t, "sx")
                                wt(t, "dx")
                                t.matmul(p1a[:, 0:wdt],
                                         oneh[:, e * P:(e + 1) * P],
                                         combE[0:NE, c - c0 + (0 if True else 0):
                                               c - c0 + wdt] if False else
                                         combE[0:NE, c:c + wdt],
                                         start=True, stop=True)
                                t.last.then_inc(inc("mm"), 1)
                            TE(f)

                            def f(v, c=c, wdt=wdt):
                                wt(v, "mm")
                                v.tensor_copy(crep[:, c:c + wdt], p1a[:, 0:wdt])
                                fire(v, "vx")
                            VE(f)

                            def f(t):
                                wt(t, "vx")
                            TE(f)
                        crep_vx[e] = cnt["vx"]

                    for ti, (c, wdt) in enumerate(tcms):
                        mm_after_h = {}

                        def emit_evac(i, mmtarget, e=e, ti=ti, c=c, wdt=wdt):
                            pb = i % 2

                            def f(s, i=i, pb=pb, wdt=wdt, mmtarget=mmtarget):
                                wt(s, "mm", mmtarget)
                                s.activation(out=hp[:, i, 0:wdt], in_=ph1[pb][:, 0:wdt],
                                             func=ACT.Silu)
                                fire(s, "sx")
                            SE(f)

                            def f(v, i=i, pb=pb, c=c, wdt=wdt, e=e, ti=ti,
                                  mmtarget=mmtarget):
                                wt(v, "mm", mmtarget)
                                wt(v, "sx")
                                v.scalar_tensor_tensor(out=hp[:, i, 0:wdt],
                                                       in0=ph3[pb][:, 0:wdt],
                                                       scalar=1.0, in1=hp[:, i, 0:wdt],
                                                       op0=ALU.mult, op1=ALU.mult)
                                v.drain()
                                if e < NE:
                                    v.tensor_tensor(out=hp[:, i, 0:wdt],
                                                    in0=hp[:, i, 0:wdt],
                                                    in1=crep[:, c:c + wdt], op=ALU.mult)
                                    v.drain()
                                v.last.then_inc(inc("vx"), 1)
                                ev_i[(e, ti, i)] = cnt["vx"]
                            VE(f)

                        for i in range(4):
                            def f(t, i=i, e=e, ti=ti, c=c, wdt=wdt):
                                if i == 0:
                                    t.wait_ge(sems["dwA" if e % 2 == 0 else "dwB"],
                                              dw13[e])
                                    if e < NE:
                                        wt(t, "vx", crep_vx[e])
                                    if ti >= 1:
                                        wt(t, "vx", ev_i[(e, ti - 1, 3)])
                                    elif e >= 1:
                                        wt(t, "vx", ev_i[(e - 1, len(tcms) - 1, 3)])
                                if i >= 2:
                                    wt(t, "vx", ev_i[(e, ti, i - 2)])
                                pb = i % 2
                                for kc in range(FC):
                                    t.matmul(ph1[pb][:, 0:wdt],
                                             w1bufs[e % 2][:, kc, i * P:(i + 1) * P],
                                             xln[:, kc, c:c + wdt],
                                             start=(kc == 0), stop=(kc == FC - 1))
                                for kc in range(FC):
                                    t.matmul(ph3[pb][:, 0:wdt],
                                             w3bufs[e % 2][:, kc, i * P:(i + 1) * P],
                                             xln[:, kc, c:c + wdt],
                                             start=(kc == 0), stop=(kc == FC - 1))
                                t.last.then_inc(inc("mm"), 1)
                                mm_after_h[i] = cnt["mm"]
                            TE(f)
                            if i >= 1:
                                emit_evac(i - 1, mm_after_h[i - 1])
                        emit_evac(3, mm_after_h[3])
                        mm_h[e] = cnt["mm"]

                        if wdt > 256:
                            a = (wdt + 1) // 2
                            hws = [(0, a), (a, wdt - a)]
                        else:
                            hws = [(0, wdt)]
                        for hf, (h0, hw) in enumerate(hws):
                            for dp in range(2):
                                def f(t, e=e, ti=ti, hf=hf, dp=dp, h0=h0, hw=hw):
                                    if hf == 0 and dp == 0:
                                        if ti == 0:
                                            t.wait_ge(sems["dw2A" if e % 2 == 0
                                                           else "dw2B"], dw2[e])
                                        wt(t, "vx", ev_i[(e, ti, 3)])
                                    prev = ysx.get(dp)
                                    if prev is not None:
                                        wt(t, "vx", prev)
                                    for dci in range(4):
                                        dc = 4 * dp + dci
                                        for i in range(4):
                                            t.matmul(pys[dp][:, dci, 0:hw],
                                                     w2bufs[e % 2][:, i,
                                                                   dc * P:(dc + 1) * P],
                                                     hp[:, i, h0:h0 + hw],
                                                     start=(i == 0), stop=(i == 3))
                                    t.last.then_inc(inc("mm"), 1)
                                TE(f)
                                mmy = cnt["mm"]

                                def f(v, dp=dp, hw=hw, mmy=mmy):
                                    wt(v, "mm", mmy)
                                    prev_da = yda.get(dp)
                                    if prev_da is not None:
                                        v.wait_ge(sems["daA" if dp == 0 else "daB"],
                                                  prev_da)
                                    v.tensor_scalar(out=yts[dp].ap()[:, :, 0:hw],
                                                    in0=pys[dp].ap()[:, :, 0:hw],
                                                    scalar1=0.5, scalar2=None,
                                                    op0=ALU.mult)
                                    fire(v, "vx")
                                VE(f)
                                ysx[dp] = cnt["vx"]
                                sx_now = cnt["vx"]

                                def f(g, c=c, h0=h0, hw=hw, dp=dp, sx_now=sx_now):
                                    wt(g, "vx", sx_now)
                                    dmago(g, "daA" if dp == 0 else "daB",
                                          xres[:, 4 * dp:4 * dp + 4,
                                               c + h0:c + h0 + hw],
                                          yts[dp].ap()[:, :, 0:hw],
                                          accum_op=ALU.add)
                                GE(f)
                                yda[dp] = cnt["daA" if dp == 0 else "daB"]
                        mm_y[e] = cnt["mm"]

                    if e + 2 < NEXP:
                        def f(g, e=e):
                            g.wait_ge(sems["mm"], mm_h[e])
                            dw13[e + 2] = load_w13(g, e + 2)
                            g.wait_ge(sems["mm"], mm_y[e])
                            dw2[e + 2] = load_w2(g, e + 2)
                        GE(f)

            join_all()

        def emit_attn(xln):
            with ExitStack() as stk4:
                t2g = stk4.enter_context(nc.sbuf_tensor("a_t2g", [64, TCORE], BF16))
                o_sb = stk4.enter_context(nc.sbuf_tensor("a_o", [P, NG, VD], BF16))
                ab_stk = ExitStack()
                r_sb = ab_stk.enter_context(nc.sbuf_tensor("a_r", [P, HEADS, TCORE], BF16))
                k_sb = ab_stk.enter_context(nc.sbuf_tensor("a_k", [P, HEADS, TCORE], BF16))
                t2w = ab_stk.enter_context(nc.sbuf_tensor("a_t2w", [64, TCORE], BF16))
                swB = ab_stk.enter_context(nc.sbuf_tensor("a_wB", [64, KD], BF16))

                def f(g):
                    dmago(g, "dw", swB[:, :], w_BT[:, :])
                GE(f)

                with ExitStack() as stk5:
                    xl_a = stk5.enter_context(nc.sbuf_tensor("aa_xla", [P, TCORE], BF16))
                    xl_b = stk5.enter_context(nc.sbuf_tensor("aa_xlb", [32, TCORE], BF16))
                    xb0 = stk5.enter_context(nc.sbuf_tensor("aa_xb0", [P, 512], BF16))
                    xb1 = stk5.enter_context(nc.sbuf_tensor("aa_xb1", [P, 512], BF16))
                    dtile = stk5.enter_context(nc.sbuf_tensor("aa_dt", [P, 512], BF16))
                    vtile = stk5.enter_context(nc.sbuf_tensor("aa_vt", [P, 512], BF16))
                    wstr = stk5.enter_context(nc.sbuf_tensor("aa_wst", [P, FC, 512], BF16))
                    sxpw1 = stk5.enter_context(nc.sbuf_tensor("aa_xpw1", [P, FC, 160], BF16))
                    sxpw2p = stk5.enter_context(nc.sbuf_tensor("aa_xpw2p", [P, D], BF16))
                    sxpw2g = stk5.enter_context(nc.sbuf_tensor("aa_xpw2g", [32, D], BF16))
                    swA = stk5.enter_context(nc.sbuf_tensor("aa_wA", [P, FC, 64], BF16))
                    sgA = stk5.enter_context(nc.sbuf_tensor("aa_gA", [P, FC, 64], BF16))
                    pj = stk5.enter_context(nc.psum_tensor("aa_pj", [P, 512], F32))
                    pjs = stk5.enter_context(nc.psum_tensor("aa_pjs", [P, 512], F32))
                    pr0 = stk5.enter_context(nc.psum_tensor("aa_pr0", [P, 512], F32))
                    pr1 = stk5.enter_context(nc.psum_tensor("aa_pr1", [P, 512], F32))
                    pr2 = stk5.enter_context(nc.psum_tensor("aa_pr2", [P, 512], F32))
                    pr3 = stk5.enter_context(nc.psum_tensor("aa_pr3", [P, 512], F32))

                    prs = [pr0, pr1, pr2, pr3]
                    xbs = [xb0, xb1]

                    def f(g):
                        dmago(g, "dw", sxpw1.ap()[:, :, :],
                              xp_W1T.rearrange("(a p) b -> p a b", p=P))
                        dmago(g, "dw", sxpw2p[:, :], xp_W2p[:, :])
                        dmago(g, "dw", sxpw2g[:, :], xp_W2g[:, :])
                        dmago(g, "dw", swA.ap()[:, :, :],
                              w_AT.rearrange("(a p) b -> p a b", p=P))
                        dmago(g, "dw", sgA.ap()[:, :, :],
                              g_AT.rearrange("(a p) b -> p a b", p=P))
                        dmago(g, "dw", wstr.ap()[:, :, :],
                              r_WT.rearrange("(a p) b -> p a b", p=P))
                    GE(f)
                    dw_r = cnt["dw"]

                    vx_xb = {}
                    for ts in range(4):
                        t0 = ts * 512
                        for fc in range(FC):
                            def f(v, ts=ts, t0=t0, fc=fc):
                                wt(v, "mm")
                                b = fc % 2
                                v.tensor_tensor(out=dtile[:, :],
                                                in0=xln[:, fc, t0:t0 + 512],
                                                in1=xln[:, fc, 1 + t0:1 + t0 + 512],
                                                op=ALU.subtract)
                                v.drain()
                                v.scalar_tensor_tensor(out=xbs[b][:, :], in0=dtile[:, :],
                                                       scalar=sb_xpmu[:, fc:fc + 1],
                                                       in1=xln[:, fc, 1 + t0:1 + t0 + 512],
                                                       op0=ALU.mult, op1=ALU.add)
                                fire(v, "vx")
                                vx_xb[(ts, fc)] = cnt["vx"]
                            VE(f)

                            def f(t, ts=ts, fc=fc):
                                wt(t, "vx", vx_xb[(ts, fc)])
                                if ts == 0 and fc == 0:
                                    wt(t, "dw")
                                b = fc % 2
                                t.matmul(pj[:, :], sxpw1[:, fc, 0:P], xbs[b][:, :],
                                         start=(fc == 0), stop=(fc == FC - 1))
                                t.matmul(pjs[0:32, :], sxpw1[:, fc, P:160], xbs[b][:, :],
                                         start=(fc == 0), stop=(fc == FC - 1))
                                t.last.then_inc(inc("mm"), 1)
                            TE(f)

                        def f(s, ts=ts, t0=t0):
                            wt(s, "mm")
                            s.activation(out=xl_a[:, t0:t0 + 512], in_=pj[:, :],
                                         func=ACT.Tanh)
                            s.drain()
                            s.activation(out=xl_b[:, t0:t0 + 512], in_=pjs[0:32, :],
                                         func=ACT.Tanh)
                            fire(s, "sx")
                        SE(f)

                        def f(t):
                            wt(t, "sx")
                        TE(f)

                    def emit_branch(n, kind, dest=None, dwwait=None, spill_base=None):
                        pjp = [pj, pjs]
                        dts = [dtile, vtile]

                        def mu_mm(t, n, fc, t0):
                            if n < 4:
                                t.matmul(pjp[fc % 2][:, :],
                                         sxpw2p[32 * n:32 * n + 32,
                                                fc * P:(fc + 1) * P],
                                         xl_a[32 * n:32 * n + 32, t0:t0 + 512],
                                         start=True, stop=True,
                                         tile_position=(32 * n, 0))
                            else:
                                t.matmul(pjp[fc % 2][:, :],
                                         sxpw2g[:, fc * P:(fc + 1) * P],
                                         xl_b[:, t0:t0 + 512],
                                         start=True, stop=True)
                            t.last.then_inc(inc("mm"), 1)

                        def proj_mm(t, kind, fc):
                            b = fc % 2
                            if kind in ("big4", "bigv"):
                                for hc in range(4):
                                    t.matmul(prs[hc][:, :],
                                             wstr[:, fc, hc * P:(hc + 1) * P],
                                             xbs[b][:, :],
                                             start=(fc == 0), stop=(fc == FC - 1))
                            else:
                                A = swA if kind == "rankw" else sgA
                                t.matmul(prs[0][0:64, :], A[:, fc, :], xbs[b][:, :],
                                         start=(fc == 0), stop=(fc == FC - 1))
                            t.last.then_inc(inc("mm"), 1)

                        for ts in range(4):
                            t0 = ts * 512
                            vx_b = {}
                            mm_mu = {}
                            mm_pj = {}

                            def f(t, n=n, t0=t0, dwwait=dwwait):
                                if dwwait is not None:
                                    t.wait_ge(sems["dw"], dwwait)
                                wt(t, "vx")
                                mu_mm(t, n, 0, t0)
                                mm_mu[0] = cnt["mm"]
                                mu_mm(t, n, 1, t0)
                                mm_mu[1] = cnt["mm"]
                            TE(f)

                            for fc in range(FC):
                                def f(v, n=n, ts=ts, t0=t0, fc=fc):
                                    b = fc % 2
                                    v.tensor_tensor(out=dts[b][:, :],
                                                    in0=xln[:, fc, t0:t0 + 512],
                                                    in1=xln[:, fc, 1 + t0:1 + t0 + 512],
                                                    op=ALU.subtract)
                                    wt(v, "mm", max(mm_mu[fc], mm_pj.get(fc - 2, 0)))
                                    v.drain()
                                    v.scalar_tensor_tensor(
                                        out=xbs[b][:, :], in0=pjp[fc % 2][:, :],
                                        scalar=sb_xbias[:, n * FC + fc:n * FC + fc + 1],
                                        in1=dts[b][:, :], op0=ALU.add, op1=ALU.mult)
                                    v.drain()
                                    v.tensor_tensor(out=xbs[b][:, :], in0=xbs[b][:, :],
                                                    in1=xln[:, fc, 1 + t0:1 + t0 + 512],
                                                    op=ALU.add)
                                    fire(v, "vx")
                                    vx_b[fc] = cnt["vx"]
                                VE(f)

                                def f(t, kind=kind, n=n, fc=fc, t0=t0):
                                    wt(t, "vx", vx_b[fc])
                                    if fc + 2 < FC:
                                        mu_mm(t, n, fc + 2, t0)
                                        mm_mu[fc + 2] = cnt["mm"]
                                    proj_mm(t, kind, fc)
                                    mm_pj[fc] = cnt["mm"]
                                TE(f)

                            mm_pr = cnt["mm"]
                            if kind == "big4":
                                def f(s, ts=ts, t0=t0, dest=dest, mm_pr=mm_pr):
                                    wt(s, "mm", mm_pr)
                                    for hc in range(4):
                                        s.activation(out=dest[:, hc, t0:t0 + 512],
                                                     in_=prs[hc][:, :], func=ACT.Copy)
                                    fire(s, "sx")
                                SE(f)
                            elif kind == "bigv":
                                def f(s, ts=ts, mm_pr=mm_pr):
                                    wt(s, "mm", mm_pr)
                                    wt(s, "dw")
                                    for hc in range(4):
                                        s.activation(out=vtile[:, 0:512] if False else
                                                     vtile[:, :],
                                                     in_=prs[hc][:, :], func=ACT.Copy)
                                        s.drain()
                                        s.nop().then_inc(inc("sx"), 1)
                                    fire(s, "sx")
                                SE(f)
                            else:
                                def f(s, ts=ts, t0=t0, dest=dest, mm_pr=mm_pr):
                                    wt(s, "mm", mm_pr)
                                    s.activation(out=dest[:, t0:t0 + 512],
                                                 in_=prs[0][0:64, :], func=ACT.Tanh)
                                    fire(s, "sx")
                                SE(f)

                            def f(t):
                                wt(t, "sx")
                            TE(f)

                    emit_branch(0, "big4", dest=r_sb, dwwait=dw_r)
                    emit_branch(1, "rankw", dest=t2w)

                    def f(g):
                        wt(g, "mm")
                        dmago(g, "dw", wstr.ap()[:, :, :],
                              k_WT.rearrange("(a p) b -> p a b", p=P))
                    GE(f)
                    dw_k = cnt["dw"]

                    emit_branch(2, "big4", dest=k_sb, dwwait=dw_k)
                    emit_branch(4, "rankg", dest=t2g)

                    for vp in range(2):
                        def f(g, vp=vp):
                            wt(g, "mm")
                            dmago(g, "dw", wstr.ap()[:, :, :],
                                  v_WT[:, vp * 512:(vp + 1) * 512]
                                  .rearrange("(a p) b -> p a b", p=P))
                        GE(f)
                        dw_v = cnt["dw"]

                        for ts in range(4):
                            t0 = ts * 512
                            for fc in range(FC):
                                def f(v, ts=ts, t0=t0, fc=fc):
                                    wt(v, "mm")
                                    b = fc % 2
                                    v.tensor_tensor(out=dtile[:, :],
                                                    in0=xln[:, fc, t0:t0 + 512],
                                                    in1=xln[:, fc, 1 + t0:1 + t0 + 512],
                                                    op=ALU.subtract)
                                    fire(v, "vx")
                                VE(f)

                                def f(t, fc=fc, t0=t0):
                                    wt(t, "vx")
                                    t.matmul(pj[:, :],
                                             sxpw2p[96:128, fc * P:(fc + 1) * P],
                                             xl_a[96:128, t0:t0 + 512],
                                             start=True, stop=True,
                                             tile_position=(96, 0))
                                    t.last.then_inc(inc("mm"), 1)
                                TE(f)
                                mm_mu = cnt["mm"]

                                def f(v, fc=fc, t0=t0, mm_mu=mm_mu):
                                    wt(v, "mm", mm_mu)
                                    b = fc % 2
                                    v.scalar_tensor_tensor(
                                        out=xbs[b][:, :], in0=pj[:, :],
                                        scalar=sb_xbias[:, 3 * FC + fc:3 * FC + fc + 1],
                                        in1=dtile[:, :], op0=ALU.add, op1=ALU.mult)
                                    v.drain()
                                    v.tensor_tensor(out=xbs[b][:, :], in0=xbs[b][:, :],
                                                    in1=xln[:, fc, 1 + t0:1 + t0 + 512],
                                                    op=ALU.add)
                                    fire(v, "vx")
                                VE(f)
                                vx_b = cnt["vx"]

                                def f(t, fc=fc, vx_b=vx_b, dw_v=dw_v):
                                    wt(t, "vx", vx_b)
                                    if fc == 0:
                                        t.wait_ge(sems["dw"], dw_v)
                                    b = fc % 2
                                    for hc in range(4):
                                        t.matmul(prs[hc][:, :],
                                                 wstr[:, fc, hc * P:(hc + 1) * P],
                                                 xbs[b][:, :],
                                                 start=(fc == 0), stop=(fc == FC - 1))
                                    t.last.then_inc(inc("mm"), 1)
                                TE(f)

                            mm_pr = cnt["mm"]
                            for hc in range(4):
                                def f(s, hc=hc, mm_pr=mm_pr):
                                    wt(s, "mm", mm_pr)
                                    wt(s, "dw")
                                    s.activation(out=vtile[:, :], in_=prs[hc][:, :],
                                                 func=ACT.Copy)
                                    fire(s, "sx")
                                SE(f)
                                sx_v = cnt["sx"]

                                def f(g, vp=vp, ts=ts, t0=t0, hc=hc, sx_v=sx_v):
                                    wt(g, "sx", sx_v)
                                    vcg = vp * 4 + hc
                                    dmago(g, "dw", v_spill.ap()[vcg, :, t0:t0 + 512],
                                          vtile[:, :])
                                GE(f)

                            def f(t):
                                wt(t, "sx")
                            TE(f)

                join_all()

                with ExitStack() as stk6:
                    qt = stk6.enter_context(nc.sbuf_tensor("bb_qt", [P, TCORE], BF16))
                    ktf = stk6.enter_context(nc.sbuf_tensor("bb_ktf", [P, 1024], BF16))
                    ktok = stk6.enter_context(nc.sbuf_tensor("bb_ktok", [P, 8, P], BF16))
                    vpre = stk6.enter_context(nc.sbuf_tensor("bb_vpre", [P, 2, 1024], BF16))
                    vtok = stk6.enter_context(nc.sbuf_tensor("bb_vtok", [P, 8, DV], BF16))
                    mid = stk6.enter_context(nc.sbuf_tensor("bb_mid", [P, NCH], F32))
                    cwc = stk6.enter_context(nc.sbuf_tensor("bb_cwc", [P, NCH], F32))
                    logf = stk6.enter_context(nc.sbuf_tensor("bb_logf", [P, NCH], F32))
                    Rf = stk6.enter_context(nc.sbuf_tensor("bb_R", [P, NCH], F32))
                    dtk = stk6.enter_context(nc.sbuf_tensor("bb_dtk", [P, NG], F32))
                    A_f = stk6.enter_context(nc.sbuf_tensor("bb_A", [P, DV], F32))
                    A_f2 = stk6.enter_context(nc.sbuf_tensor("bb_A2", [P, DV], F32))
                    A_b = stk6.enter_context(nc.sbuf_tensor("bb_Ab", [P, DV], BF16))
                    STs = stk6.enter_context(nc.sbuf_tensor("bb_ST", [P, P], BF16))
                    Dd = stk6.enter_context(nc.sbuf_tensor("bb_Dd", [P, P], BF16))
                    Pv = stk6.enter_context(nc.sbuf_tensor("bb_P", [P, 1], F32))
                    scr0 = stk6.enter_context(nc.sbuf_tensor("bb_s0", [P, 1024], F32))
                    scr1 = stk6.enter_context(nc.sbuf_tensor("bb_s1", [P, 1024], F32))
                    scr2 = stk6.enter_context(nc.sbuf_tensor("bb_s2", [P, 1024], F32))
                    pw = stk6.enter_context(nc.psum_tensor("bb_pw", [P, 1024], F32))
                    pt = stk6.enter_context(nc.psum_tensor("bb_pt", [P, 512], F32))
                    ptb = stk6.enter_context(nc.psum_tensor("bb_ptb", [P, 1024], BF16))
                    pST = stk6.enter_context(nc.psum_tensor("bb_pST", [P, 512], F32))
                    pO = stk6.enter_context(nc.psum_tensor("bb_pO", [P, 512], F32))
                    pU = stk6.enter_context(nc.psum_tensor("bb_pU", [P, 512], F32))
                    pd = stk6.enter_context(nc.psum_tensor("bb_pd", [P, 512], F32))

                    cc_done = {}

                    def emit_half(h, half):
                        hb = half * 1024
                        def f(t, h=h, hb=hb):
                            wt(t, "vx")
                            wt(t, "sx")
                            for q in range(2):
                                t.matmul(pw[:, q * 512:(q + 1) * 512],
                                         swB[:, h * P:(h + 1) * P],
                                         t2w[:, hb + q * 512: hb + (q + 1) * 512],
                                         start=True, stop=True)
                            t.last.then_inc(inc("mm"), 1)
                        TE(f)

                        def f(s, h=h):
                            wt(s, "mm")
                            s.activation(out=scr0[:, :], in_=pw[:, :], func=ACT.Exp,
                                         bias=sb_wb[:, h:h + 1])
                            fire(s, "sx")
                        SE(f)

                        def f(sy, h=h, hb=hb):
                            wt(sy, "mm")
                            dmago(sy, "dx",
                                  vpre.ap()[:, :, :],
                                  v_spill.ap()[2 * h:2 * h + 2, :, hb:hb + 1024]
                                  .rearrange("a p b -> p a b"))
                        SY(f)
                        dx_v = cnt["dx"]

                        for gp in range(4):
                            g0 = gp * 2
                            def f(t, g0=g0, gp=gp):
                                if gp == 0:
                                    wt(t, "sx")
                                wt(t, "vx")
                                for j in range(2):
                                    t.transpose(pt[:, j * P:(j + 1) * P],
                                                scr0[:, (g0 + j) * P:(g0 + j + 1) * P],
                                                sb_idf[:, :])
                                t.last.then_inc(inc("mm"), 1)
                            TE(f)

                            def f(v, g0=g0):
                                wt(v, "mm")
                                for j in range(2):
                                    v.tensor_copy(scr1[:, (g0 + j) * P:(g0 + j + 1) * P],
                                                  pt[:, j * P:(j + 1) * P])
                                fire(v, "vx")
                            VE(f)

                            def f(t, g0=g0):
                                wt(t, "vx")
                                for j in range(2):
                                    t.matmul(pt[:, j * P:(j + 1) * P], sb_tri[:, :],
                                             scr1[:, (g0 + j) * P:(g0 + j + 1) * P],
                                             start=True, stop=True)
                                t.last.then_inc(inc("mm"), 1)
                            TE(f)

                            def f(v, g0=g0):
                                wt(v, "mm")
                                for j in range(2):
                                    v.tensor_copy(scr1[:, (g0 + j) * P:(g0 + j + 1) * P],
                                                  pt[:, j * P:(j + 1) * P])
                                fire(v, "vx")
                            VE(f)

                            def f(t, g0=g0):
                                wt(t, "vx")
                                for j in range(2):
                                    t.transpose(pt[:, j * P:(j + 1) * P],
                                                scr1[:, (g0 + j) * P:(g0 + j + 1) * P],
                                                sb_idf[:, :])
                                t.last.then_inc(inc("mm"), 1)
                            TE(f)

                            def f(v, g0=g0):
                                wt(v, "mm")
                                for j in range(2):
                                    v.tensor_copy(scr2[:, (g0 + j) * P:(g0 + j + 1) * P],
                                                  pt[:, j * P:(j + 1) * P])
                                fire(v, "vx")
                            VE(f)

                        def f(t):
                            wt(t, "vx")
                        TE(f)


                        def f(v, half=half):
                            wt(v, "vx")
                            cw3 = scr2.ap().rearrange("p (c t) -> p c t", t=CH)
                            v.tensor_copy(mid.ap()[:, half * 16:half * 16 + 16]
                                          .unsqueeze(2), cw3[:, :, 31:32])
                            v.tensor_copy(cwc.ap()[:, half * 16:half * 16 + 16]
                                          .unsqueeze(2), cw3[:, :, 63:64])
                            v.drain()
                            midv = mid.ap()[:, half * 16:half * 16 + 16] \
                                .unsqueeze(2).to_broadcast([P, 16, CH])
                            s1v = scr1.ap().rearrange("p (c t) -> p c t", t=CH)
                            s0v = scr0.ap().rearrange("p (c t) -> p c t", t=CH)
                            s2v = scr2.ap().rearrange("p (c t) -> p c t", t=CH)
                            v.tensor_tensor(out=scr1[:, :], in0=scr2[:, :],
                                            in1=scr0[:, :], op=ALU.add)
                            v.drain()
                            v.tensor_tensor(out=s1v, in0=s1v, in1=midv,
                                            op=ALU.subtract)
                            v.drain()
                            v.tensor_tensor(out=s0v, in0=midv, in1=s2v,
                                            op=ALU.subtract)
                            fire(v, "vx")
                        VE(f)

                        def f(s):
                            wt(s, "vx")
                            s.activation(out=scr1[:, :], in_=scr1[:, :], func=ACT.Exp)
                            s.drain()
                            s.activation(out=scr0[:, :], in_=scr0[:, :], func=ACT.Exp)
                            fire(s, "sx")
                        SE(f)

                        def f(v, h=h, hb=hb):
                            wt(v, "sx")
                            v.tensor_tensor(out=qt[:, hb:hb + 1024],
                                            in0=r_sb[:, h, hb:hb + 1024],
                                            in1=scr1[:, :], op=ALU.mult)
                            v.drain()
                            v.tensor_tensor(out=ktf[:, :],
                                            in0=k_sb[:, h, hb:hb + 1024],
                                            in1=scr0[:, :], op=ALU.mult)
                            fire(v, "vx")
                        VE(f)

                        vx_tk = {}
                        for gg in range(8):
                            def f(t, gg=gg, dx_v=dx_v):
                                if gg == 0:
                                    wt(t, "vx")
                                    t.wait_ge(sems["dx"], dx_v)
                                if gg >= 1:
                                    wt(t, "vx", vx_tk[gg - 1])
                                t.transpose(ptb[:, 0:P], ktf[:, gg * P:(gg + 1) * P],
                                            sb_idb[:, :])
                                t.transpose(ptb[:, P:2 * P],
                                            vpre[:, 0, gg * P:(gg + 1) * P],
                                            sb_idb[:, :])
                                t.transpose(ptb[:, 2 * P:3 * P],
                                            vpre[:, 1, gg * P:(gg + 1) * P],
                                            sb_idb[:, :])
                                t.last.then_inc(inc("mm"), 1)
                            TE(f)
                            mm_tk = cnt["mm"]

                            def f(v, gg=gg, mm_tk=mm_tk):
                                wt(v, "mm", mm_tk)
                                v.tensor_copy(ktok[:, gg, :], ptb[:, 0:P])
                                v.tensor_copy(vtok[:, gg, 0:P], ptb[:, P:2 * P])
                                v.tensor_copy(vtok[:, gg, P:2 * P], ptb[:, 2 * P:3 * P])
                                fire(v, "vx")
                                vx_tk[gg] = cnt["vx"]
                            VE(f)

                        def f(v, h=h, hb=hb):
                            wt(v, "vx")
                            v.tensor_scalar(out=scr0[:, :], in0=r_sb[:, h, hb:hb + 1024],
                                            scalar1=sb_u[:, h:h + 1], scalar2=None,
                                            op0=ALU.mult)
                            v.drain()
                            v.tensor_tensor(out=scr0[:, :], in0=scr0[:, :],
                                            in1=k_sb[:, h, hb:hb + 1024], op=ALU.mult)
                            fire(v, "vx")
                        VE(f)

                        def f(t, half=half):
                            wt(t, "vx")
                            for gg in range(8):
                                t.matmul(pd[:, half * 8 + gg:half * 8 + gg + 1],
                                         scr0[:, gg * P:(gg + 1) * P],
                                         sb_onescol[:, :], start=True, stop=True)
                            t.last.then_inc(inc("mm"), 1)
                        TE(f)

                        def f(v, half=half):
                            wt(v, "mm")
                            v.tensor_copy(dtk[:, half * 8:half * 8 + 8],
                                          pd[:, half * 8:half * 8 + 8])
                            fire(v, "vx")
                        VE(f)

                        for gg in range(8):
                            g = half * 8 + gg

                            def f(t, gg=gg):
                                wt(t, "vx")
                                wt(t, "sx")
                                t.matmul(pST[:, 0:P], ktf[:, gg * P:(gg + 1) * P],
                                         qt[:, (half * 8 + gg) * P:
                                            (half * 8 + gg + 1) * P],
                                         start=True, stop=True)
                                t.last.then_inc(inc("mm"), 1)
                            TE(f)
                            mm_st = cnt["mm"]

                            def f(v, g=g, mm_st=mm_st):
                                wt(v, "mm", mm_st)
                                v.select(out=STs[:, :], mask=sb_mask[:, :],
                                         on_true=pST[:, 0:P], on_false=sb_zero[:, :],
                                         add_drain=True)
                                v.drain()
                                v.tensor_scalar(out=Dd[:, :], in0=sb_idb[:, :],
                                                scalar1=dtk[:, g:g + 1], scalar2=None,
                                                op0=ALU.mult)
                                v.drain()
                                v.tensor_tensor(out=STs[:, :], in0=STs[:, :],
                                                in1=Dd[:, :], op=ALU.add)
                                fire(v, "vx")
                            VE(f)
                            vx_st = cnt["vx"]

                            def f(t, g=g, gg=gg, vx_st=vx_st):
                                wt(t, "vx", vx_st)
                                wt(t, "sx")
                                t.matmul(pO[:, 0:DV], STs[:, :], vtok[:, gg, :],
                                         start=True, stop=False, skip_group_check=True)
                                t.matmul(pO[0:CH, 0:DV], qt[:, g * P:g * P + CH],
                                         A_b[:, :], start=False, stop=False,
                                         skip_group_check=True)
                                t.matmul(pU[:, 0:DV], ktok[0:CH, gg, :],
                                         vtok[0:CH, gg, :], start=True, stop=True,
                                         skip_group_check=True)
                                t.last.then_inc(inc("mm"), 1)
                            TE(f)
                            mm_u0 = cnt["mm"]

                            def f(v, mm_u0=mm_u0):
                                wt(v, "mm", mm_u0)
                                v.scalar_tensor_tensor(out=A_f2[:, :], in0=A_f[:, :],
                                                       scalar=1.0, in1=pU[:, 0:DV],
                                                       op0=ALU.mult, op1=ALU.add)
                                fire(v, "vx")
                            VE(f)

                            def f(v, g=g):
                                v.drain()
                                v.tensor_scalar(out=A_f[:, :], in0=A_f2[:, :],
                                                scalar1=logf[:, 2 * g:2 * g + 1],
                                                scalar2=None, op0=ALU.mult)
                                v.drain()
                                v.tensor_copy(A_b[:, :], A_f[:, :])
                                fire(v, "vx")
                            VE(f)
                            sx_a0 = cnt["vx"]

                            def f(t, g=g, gg=gg, sx_a0=sx_a0):
                                wt(t, "vx", sx_a0)
                                t.matmul(pO[CH:P, 0:DV],
                                         qt[:, g * P + CH:(g + 1) * P], A_b[:, :],
                                         start=False, stop=True, skip_group_check=True)
                                t.matmul(pU[:, 0:DV], ktok[CH:P, gg, :],
                                         vtok[CH:P, gg, :], start=True, stop=True,
                                         skip_group_check=True)
                                t.last.then_inc(inc("mm"), 1)
                            TE(f)
                            mm_u1 = cnt["mm"]

                            def f(v, g=g, h=h, mm_u1=mm_u1):
                                wt(v, "mm", mm_u1)
                                v.scalar_tensor_tensor(out=A_f2[:, :], in0=A_f[:, :],
                                                       scalar=1.0, in1=pU[:, 0:DV],
                                                       op0=ALU.mult, op1=ALU.add)
                                v.drain()
                                v.tensor_copy(o_sb[:, g, h * DV:(h + 1) * DV], pO[:, 0:DV])
                                fire(v, "vx")
                            VE(f)

                            def f(v, g=g):
                                v.drain()
                                v.tensor_scalar(out=A_f[:, :], in0=A_f2[:, :],
                                                scalar1=logf[:, 2 * g + 1:2 * g + 2],
                                                scalar2=None, op0=ALU.mult)
                                v.drain()
                                v.tensor_copy(A_b[:, :], A_f[:, :])
                                fire(v, "vx")
                            VE(f)

                    def emit_passA(h):
                        pass

                    for h in range(HEADS):
                        for half in range(2):
                            hb = half * 1024

                            def f(t, h=h, hb=hb):
                                wt(t, "vx")
                                wt(t, "sx")
                                for q in range(2):
                                    t.matmul(pw[:, q * 512:(q + 1) * 512],
                                             swB[:, h * P:(h + 1) * P],
                                             t2w[:, hb + q * 512: hb + (q + 1) * 512],
                                             start=True, stop=True)
                                t.last.then_inc(inc("mm"), 1)
                            TE(f)

                            def f(s, h=h):
                                wt(s, "mm")
                                s.activation(out=scr0[:, :], in_=pw[:, :], func=ACT.Exp,
                                             bias=sb_wb[:, h:h + 1])
                                fire(s, "sx")
                            SE(f)

                            for gp in range(4):
                                g0 = gp * 2
                                def f(t, g0=g0, gp=gp):
                                    if gp == 0:
                                        wt(t, "sx")
                                    wt(t, "vx")
                                    for j in range(2):
                                        t.transpose(pt[:, j * P:(j + 1) * P],
                                                    scr0[:, (g0 + j) * P:(g0 + j + 1) * P],
                                                    sb_idf[:, :])
                                    t.last.then_inc(inc("mm"), 1)
                                TE(f)

                                def f(v, g0=g0):
                                    wt(v, "mm")
                                    for j in range(2):
                                        v.tensor_copy(scr1[:, (g0 + j) * P:(g0 + j + 1) * P],
                                                      pt[:, j * P:(j + 1) * P])
                                    fire(v, "vx")
                                VE(f)

                                def f(t, g0=g0):
                                    wt(t, "vx")
                                    for j in range(2):
                                        t.matmul(pt[:, j * P:(j + 1) * P], sb_tri[:, :],
                                                 scr1[:, (g0 + j) * P:(g0 + j + 1) * P],
                                                 start=True, stop=True)
                                    t.last.then_inc(inc("mm"), 1)
                                TE(f)

                                def f(v, g0=g0):
                                    wt(v, "mm")
                                    for j in range(2):
                                        v.tensor_copy(scr1[:, (g0 + j) * P:(g0 + j + 1) * P],
                                                      pt[:, j * P:(j + 1) * P])
                                    fire(v, "vx")
                                VE(f)

                                def f(t, g0=g0):
                                    wt(t, "vx")
                                    for j in range(2):
                                        t.transpose(pt[:, j * P:(j + 1) * P],
                                                    scr1[:, (g0 + j) * P:(g0 + j + 1) * P],
                                                    sb_idf[:, :])
                                    t.last.then_inc(inc("mm"), 1)
                                TE(f)

                                def f(v, g0=g0):
                                    wt(v, "mm")
                                    for j in range(2):
                                        v.tensor_copy(scr2[:, (g0 + j) * P:(g0 + j + 1) * P],
                                                      pt[:, j * P:(j + 1) * P])
                                    fire(v, "vx")
                                VE(f)

                            def f(t):
                                wt(t, "vx")
                            TE(f)


                            def f(v, half=half):
                                wt(v, "vx")
                                cw3 = scr2.ap().rearrange("p (c t) -> p c t", t=CH)
                                v.tensor_copy(mid.ap()[:, half * 16:half * 16 + 16]
                                              .unsqueeze(2), cw3[:, :, 31:32])
                                v.tensor_copy(cwc.ap()[:, half * 16:half * 16 + 16]
                                              .unsqueeze(2), cw3[:, :, 63:64])
                                fire(v, "vx")
                            VE(f)

                        def f(v):
                            wt(v, "vx")
                            v.tensor_tensor(out=logf[:, :], in0=cwc[:, :], in1=mid[:, :],
                                            op=ALU.subtract)
                            v.drain()
                            v.tensor_tensor(out=logf[:, 0:NCH - 1],
                                            in0=logf[:, 0:NCH - 1],
                                            in1=mid[:, 1:NCH], op=ALU.add)
                            v.drain()
                            v.tensor_tensor_scan(out=scr0[:, 0:NCH], data0=logf[:, :],
                                                 data1=logf[:, :], initial=0.0,
                                                 op0=ALU.bypass, op1=ALU.add)
                            v.drain()
                            v.tensor_copy(Rf[:, 0:1], mid[:, 0:1])
                            v.drain()
                            v.tensor_tensor(out=Rf[:, 1:NCH], in0=scr0[:, 0:NCH - 1],
                                            in1=mid[:, 0:1].to_broadcast([P, NCH - 1]),
                                            op=ALU.add)
                            v.drain()
                            v.memset(A_f[:, :], 0.0)
                            v.memset(A_b[:, :], 0.0)
                            fire(v, "vx")
                        VE(f)

                        def f(s):
                            wt(s, "vx")
                            s.activation(out=logf[:, :], in_=logf[:, :], func=ACT.Exp)
                            s.drain()
                            s.activation(out=Rf[:, :], in_=Rf[:, :], func=ACT.Exp)
                            fire(s, "sx")
                        SE(f)

                        for half in range(2):
                            emit_half(h, half)

                        def f(v):
                            wt(v, "sx")
                            v.tensor_reduce(out=Pv[:, :], in_=cwc[:, :], axis=AX,
                                            op=ALU.add)
                            fire(v, "vx")
                        VE(f)

                        def f(s):
                            wt(s, "vx")
                            s.activation(out=Pv[:, :], in_=Pv[:, :], func=ACT.Exp)
                            fire(s, "sx")
                        SE(f)

                        def f(g, h=h):
                            wt(g, "sx")
                            dmago(g, "dw", cc_in.ap()[h, :, 0:DV], A_f[:, :])
                            g.dma_start(out=cc_in.ap()[h, :, DV:DV + 1], in_=Pv[:, :],
                                        allow_slow_non_contiguous=True) \
                                .then_inc(inc("dw", 16), 16)
                        GE(f)

                        def f(g, h=h):
                            wt(g, "dw")
                            g.collective_compute(
                                "AllGather", ALU.bypass,
                                replica_groups=[[0, 1, 2, 3], [4, 5, 6, 7]],
                                ins=[cc_in.ap()[h, :, :].opt()],
                                outs=[cc_out.ap()[h, :, :, :].opt()],
                            ).then_inc(inc("cc"), 1)
                        GE(f)
                        cc_done[h] = cnt["cc"]

                        cstk = ExitStack()
                        Gall = cstk.enter_context(nc.sbuf_tensor("bb_gal%d" % h, [P, 4, DV + 1], BF16))
                        Hin = cstk.enter_context(nc.sbuf_tensor("bb_hin%d" % h, [P, DV], F32))
                        Hcb = cstk.enter_context(nc.sbuf_tensor("bb_hcb%d" % h, [P, DV], BF16))
                        Ppr = cstk.enter_context(nc.sbuf_tensor("bb_pp%d" % h, [P, 4], F32))
                        def f(g, h=h):
                            g.wait_ge(sems["cc"], cc_done[h])
                            dmago(g, "dw", Gall.ap()[:, :, :],
                                  cc_out.ap()[h, :, :, :].rearrange("r p c -> p r c"))
                        GE(f)

                        def f(v):
                            wt(v, "dw")
                            for j in range(4):
                                v.tensor_scalar(out=Ppr[:, j:j + 1],
                                                in0=Gall[:, j, DV:DV + 1],
                                                scalar1=sb_rank[:, j:j + 1],
                                                scalar2=None, op0=ALU.mult)
                                v.drain()
                                v.tensor_tensor(out=Ppr[:, j:j + 1],
                                                in0=Ppr[:, j:j + 1],
                                                in1=sb_rank[:, 4 + j:5 + j], op=ALU.add)
                                v.drain()
                            v.tensor_scalar(out=Hin[:, :], in0=Gall[:, 0, 0:DV],
                                            scalar1=sb_rank[:, 0:1], scalar2=None,
                                            op0=ALU.mult)
                            v.drain()
                            for j in range(1, 4):
                                v.tensor_scalar(out=Hin[:, :], in0=Hin[:, :],
                                                scalar1=Ppr[:, j:j + 1], scalar2=None,
                                                op0=ALU.mult)
                                v.drain()
                                v.scalar_tensor_tensor(out=Hin[:, :],
                                                       in0=Gall[:, j, 0:DV],
                                                       scalar=sb_rank[:, j:j + 1],
                                                       in1=Hin[:, :],
                                                       op0=ALU.mult, op1=ALU.add)
                                v.drain()
                            fire(v, "vx")
                        VE(f)

                        for g in range(NG):
                            for c2 in range(2):
                                ch = 2 * g + c2

                                def f(v, ch=ch):
                                    wt(v, "mm")
                                    v.tensor_scalar(out=Hcb[:, :], in0=Hin[:, :],
                                                    scalar1=Rf[:, ch:ch + 1],
                                                    scalar2=None, op0=ALU.mult)
                                    fire(v, "vx")
                                VE(f)

                                def f(t, g=g, c2=c2):
                                    wt(t, "vx")
                                    t.matmul(pO[c2 * CH:(c2 + 1) * CH, 0:DV],
                                             qt[:, g * P + c2 * CH:
                                                g * P + (c2 + 1) * CH],
                                             Hcb[:, :],
                                             start=True, stop=True,
                                             skip_group_check=True)
                                    t.last.then_inc(inc("mm"), 1)
                                TE(f)

                            def f(v, g=g, h=h):
                                wt(v, "mm")
                                v.tensor_tensor(out=o_sb[:, g, h * DV:(h + 1) * DV],
                                                in0=o_sb[:, g, h * DV:(h + 1) * DV],
                                                in1=pO[:, 0:DV], op=ALU.add)
                                fire(v, "vx")
                            VE(f)

                            def f(t):
                                wt(t, "vx")
                            TE(f)

                        cstk.close()

                join_all()
                if DEBUG_ATTN:
                    def f(g):
                        dmago(g, "dw", dbg_qt[:, :], qt[:, :])
                        dmago(g, "dw", dbg_ktf[:, :], ktf[:, :])
                        dmago(g, "dw", dbg_A[:, :], A_f[:, :])
                        dmago(g, "dw", dbg_mid[:, :], mid[:, :])
                        dmago(g, "dw", dbg_cwc[:, :], cwc[:, :])
                        dmago(g, "dw", dbg_logf[:, :], logf[:, :])
                        dmago(g, "dw", dbg_dtk[:, :], dtk[:, :])
                        dmago(g, "dw", dbg_ktok[:, :],
                              ktok.ap().rearrange("p a b -> p (a b)"))
                        dmago(g, "dw", dbg_vtok[:, :],
                              vtok.ap().rearrange("p a b -> p (a b)"))
                        dmago(g, "dw", dbg_o[:, :],
                              o_sb.ap().rearrange("p a b -> p (a b)"))
                        dmago(g, "dw", dbg_r[:, :],
                              r_sb.ap().rearrange("p a b -> p (a b)"))
                        dmago(g, "dw", dbg_k[:, :],
                              k_sb.ap().rearrange("p a b -> p (a b)"))
                        dmago(g, "dw", dbg_t2w[:, :], t2w[:, :])
                        dmago(g, "dw", dbg_t2g[:, :], t2g[:, :])
                        dmago(g, "dw", dbg_v.rearrange("a p b -> (a p) b"),
                              v_spill.ap().rearrange("a p b -> (a p) b"))
                    GE(f)
                    join_all()
                ab_stk.close()

                with ExitStack() as stk7:
                    gsil = stk7.enter_context(nc.sbuf_tensor("cc_gs", [P, VD], BF16))
                    csq = stk7.enter_context(nc.sbuf_tensor("cc_sq", [P, VD], F32))
                    cst = stk7.enter_context(nc.sbuf_tensor("cc_st", [P, 6], F32))
                    ogf = stk7.enter_context(nc.sbuf_tensor("cc_og", [P, FC, TCORE], BF16))
                    wo = stk7.enter_context(nc.sbuf_tensor("cc_wo", [P, FC, D], BF16))
                    cgnw = stk7.enter_context(nc.sbuf_tensor("cc_gnw", [P, VD], F32))
                    cgnb = stk7.enter_context(nc.sbuf_tensor("cc_gnb", [P, VD], F32))
                    cgbr = stk7.enter_context(nc.sbuf_tensor("cc_gbr", [P, VD], F32))
                    sgB = stk7.enter_context(nc.sbuf_tensor("cc_gB", [64, VD], BF16))
                    pg0 = stk7.enter_context(nc.psum_tensor("cc_pg0", [P, 512], F32))
                    pg1 = stk7.enter_context(nc.psum_tensor("cc_pg1", [P, 512], F32))
                    cpt = stk7.enter_context(nc.psum_tensor("cc_pt", [P, 1024], BF16))
                    cpx = stk7.enter_context(nc.psum_tensor("cc_px", [P, 512], F32))

                    def f(sy):
                        dmago(sy, "dx", cgnw[:, :], gnw_rep[:, :])
                        dmago(sy, "dx", cgnb[:, :], gnb_rep[:, :])
                        dmago(sy, "dx", cgbr[:, :], gb_rep[:, :])
                    SY(f)

                    def f(g):
                        dmago(g, "dw", sgB[:, :], g_BT[:, :])
                        dmago(g, "dw", wo.ap()[:, :, :],
                              o_WT.rearrange("(a p) b -> p a b", p=P))
                    GE(f)

                    for g in range(NG):
                        def f(s, g=g):
                            if g == 0:
                                wt(s, "dx")
                                wt(s, "dw")
                            wt(s, "vx")
                            s.activation(out=csq[:, :], in_=o_sb[:, g, :],
                                         func=ACT.Square)
                            fire(s, "sx")
                        SE(f)

                        def f(v, g=g):
                            wt(v, "sx")
                            v.tensor_reduce(out=cst[:, 0:1], in_=o_sb[:, g, :],
                                            axis=AX, op=ALU.add)
                            v.tensor_reduce(out=cst[:, 1:2], in_=csq[:, :], axis=AX,
                                            op=ALU.add)
                            v.drain()
                            v.tensor_scalar(out=cst[:, 2:3], in0=cst[:, 0:1],
                                            scalar1=1.0 / VD, scalar2=None,
                                            op0=ALU.mult)
                            v.tensor_scalar(out=cst[:, 3:4], in0=cst[:, 1:2],
                                            scalar1=1.0 / VD, scalar2=None,
                                            op0=ALU.mult)
                            v.drain()
                            v.tensor_tensor(out=cst[:, 4:5], in0=cst[:, 2:3],
                                            in1=cst[:, 2:3], op=ALU.mult)
                            v.drain()
                            v.tensor_tensor(out=cst[:, 4:5], in0=cst[:, 3:4],
                                            in1=cst[:, 4:5], op=ALU.subtract)
                            fire(v, "vx")
                        VE(f)

                        def f(s, g=g):
                            wt(s, "vx")
                            s.activation(out=cst[:, 5:6], in_=cst[:, 4:5], func=ACT.Ln,
                                         bias=sb_eps[:, :])
                            s.drain()
                            s.activation(out=cst[:, 5:6], in_=cst[:, 5:6], func=ACT.Exp,
                                         scale=sb_nh[:, :])
                            fire(s, "sx")
                        SE(f)

                        def f(t, g=g):
                            wt(t, "sx")
                            t.matmul(pg0[:, :], t2g[:, g * P:(g + 1) * P],
                                     sgB[:, 0:512], start=True, stop=True)
                            t.matmul(pg1[:, :], t2g[:, g * P:(g + 1) * P],
                                     sgB[:, 512:1024], start=True, stop=True)
                            t.last.then_inc(inc("mm"), 1)
                        TE(f)

                        def f(v, g=g):
                            wt(v, "mm")
                            v.tensor_tensor(out=csq[:, 0:512], in0=pg0[:, :],
                                            in1=cgbr[:, 0:512], op=ALU.add)
                            v.tensor_tensor(out=csq[:, 512:1024], in0=pg1[:, :],
                                            in1=cgbr[:, 512:1024], op=ALU.add)
                            fire(v, "vx")
                        VE(f)

                        def f(s, g=g):
                            wt(s, "vx")
                            s.activation(out=gsil[:, :], in_=csq[:, :], func=ACT.Silu)
                            fire(s, "sx")
                        SE(f)

                        def f(v, g=g):
                            wt(v, "sx")
                            v.tensor_scalar(out=o_sb[:, g, :], in0=o_sb[:, g, :],
                                            scalar1=cst[:, 2:3], scalar2=cst[:, 5:6],
                                            op0=ALU.subtract, op1=ALU.mult)
                            v.drain()
                            v.tensor_tensor(out=o_sb[:, g, :], in0=o_sb[:, g, :],
                                            in1=cgnw[:, :], op=ALU.mult)
                            v.drain()
                            v.tensor_tensor(out=o_sb[:, g, :], in0=o_sb[:, g, :],
                                            in1=cgnb[:, :], op=ALU.add)
                            v.drain()
                            v.tensor_tensor(out=o_sb[:, g, :], in0=o_sb[:, g, :],
                                            in1=gsil[:, :], op=ALU.mult)
                            fire(v, "vx")
                        VE(f)

                        for vc in range(FC):
                            def f(t, g=g, vc=vc):
                                wt(t, "vx")
                                t.transpose(cpt[:, 0:P], o_sb[:, g, vc * P:(vc + 1) * P],
                                            sb_idb[:, :])
                                t.last.then_inc(inc("mm"), 1)
                            TE(f)

                            def f(v, g=g, vc=vc):
                                wt(v, "mm")
                                v.tensor_copy(ogf[:, vc, g * P:(g + 1) * P], cpt[:, 0:P])
                                fire(v, "vx")
                            VE(f)

                            def f(t):
                                wt(t, "vx")
                            TE(f)

                    vx_ow = {}
                    for dc in range(FC):
                        for ts in range(4):
                            t0 = ts * 512
                            idx = dc * 4 + ts

                            def f(t, dc=dc, t0=t0, idx=idx):
                                if idx == 0:
                                    wt(t, "dw")
                                    wt(t, "vx")
                                else:
                                    wt(t, "vx", vx_ow[idx - 1])
                                for vc in range(FC):
                                    t.matmul(cpx[:, :], wo[:, vc, dc * P:(dc + 1) * P],
                                             ogf[:, vc, t0:t0 + 512],
                                             start=(vc == 0), stop=(vc == FC - 1))
                                t.last.then_inc(inc("mm"), 1)
                            TE(f)

                            def f(v, dc=dc, t0=t0, idx=idx):
                                wt(v, "mm")
                                v.tensor_tensor(out=xres[:, dc, 1 + t0:1 + t0 + 512],
                                                in0=xres[:, dc, 1 + t0:1 + t0 + 512],
                                                in1=cpx[:, :], op=ALU.add)
                                fire(v, "vx")
                                vx_ow[idx] = cnt["vx"]
                            VE(f)

            join_all()

        def emit_store():
            with ExitStack() as stk8:
                toko = stk8.enter_context(nc.sbuf_tensor("st_tok", [P, D], F32))
                pso = stk8.enter_context(nc.psum_tensor("st_ps", [P, 512], F32))
                dx_st = {}
                for tt in range(NG):
                    for fc in range(FC):
                        def f(t, tt=tt, fc=fc):
                            if fc == 0 and tt >= 1:
                                wt(t, "dx", dx_st[tt - 1])
                            elif fc > 0:
                                wt(t, "vx")
                            t.transpose(pso[:, 0:P],
                                        xres[:, fc, 1 + tt * P:1 + (tt + 1) * P],
                                        sb_idf[:, :])
                            t.last.then_inc(inc("mm"), 1)
                        TE(f)

                        def f(v, fc=fc):
                            wt(v, "mm")
                            v.tensor_copy(toko[:, fc * P:(fc + 1) * P], pso[:, 0:P])
                            fire(v, "vx")
                        VE(f)

                        def f(t):
                            wt(t, "vx")
                        TE(f)

                    def f(sy, tt=tt):
                        wt(sy, "vx")
                        dmago(sy, "dx", out_t[tt * P:(tt + 1) * P, :], toko[:, :])
                        dx_st[tt] = cnt["dx"]
                    SY(f)

                def f(sy):
                    wt(sy, "dx")
                SY(f)

        with nc.sbuf_tensor("xln1", [P, FC, W], BF16) as xln1:
            if "p1" in phases:
                emit_load_ln1(xln1)
            if "ffn1" in phases:
                emit_ffn(xln1, "f1", 0, W)
            if "ln2" in phases:
                emit_ln(xln1, 0, W, "A")
            if "attn" in phases:
                emit_attn(xln1)
        with nc.sbuf_tensor("xln2", [P, FC, W], BF16) as xln2:
            if "ln3" in phases:
                emit_ln(xln2, 1, W, "B")
            if "ffn2" in phases:
                emit_ffn(xln2, "f2", 1, W)
        if "store" in phases or not phases or True:
            pass
        if "nostore" not in phases:
            emit_store()

    return nc


_NC_CACHE = None


def kernel(**inputs):
    global _NC_CACHE
    xfull = np.asarray(inputs["L0_attn_hidden_states"], dtype=np.float32)
    B, T, _ = xfull.shape
    xf = np.ascontiguousarray(xfull.reshape(B * T, D))

    def t(a):
        return np.ascontiguousarray(np.asarray(a, dtype=np.float32).T)

    i = inputs
    w1 = np.asarray(i["L0_ffn_w1"], np.float32)
    w2 = np.asarray(i["L0_ffn_w2"], np.float32)
    w3 = np.asarray(i["L0_ffn_w3"], np.float32)
    common = {
        "gate_wT": t(i["L0_ffn_gate_w"]),
        "w1T": np.ascontiguousarray(w1.transpose(0, 2, 1)),
        "w3T": np.ascontiguousarray(w3.transpose(0, 2, 1)),
        "w2T": np.ascontiguousarray(w2.transpose(0, 2, 1)),
        "ws1T": t(i["L0_ffn_ws1"]),
        "ws3T": t(i["L0_ffn_ws3"]),
        "ws2T": t(i["L0_ffn_ws2"]),
        "xp_W1T": t(i["L0_attn_xp_W1"]),
        "r_WT": t(i["L0_attn_r_W"]),
        "k_WT": t(i["L0_attn_k_W"]),
        "v_WT": t(i["L0_attn_v_W"]),
        "w_AT": t(i["L0_attn_w_A"]),
        "w_BT": t(i["L0_attn_w_B"]),
        "g_AT": t(i["L0_attn_g_A"]),
        "g_BT": t(i["L0_attn_g_B"]),
        "o_WT": t(i["L0_attn_o_W"]),
    }
    xpW2T = t(i["L0_attn_xp_W2"])
    common["xp_W2p"] = np.ascontiguousarray(xpW2T[0:128, :])
    common["xp_W2g"] = np.ascontiguousarray(xpW2T[128:160, :])

    def fpack(vec, n):
        v = np.asarray(vec, np.float32).reshape(n, P)
        return np.ascontiguousarray(v.T)

    common["xp_mu_p"] = fpack(i["L0_attn_xp_mu"], FC)
    xb = np.asarray(i["L0_attn_x_bias"], np.float32)
    xbp = np.stack([fpack(xb[n], FC) for n in range(5)], 1)
    common["xbias_p"] = np.ascontiguousarray(xbp.reshape(P, 5 * FC))
    common["wb_p"] = fpack(i["L0_attn_w_b"], HEADS)
    common["bonus_p"] = np.ascontiguousarray(np.asarray(i["L0_attn_bonus"], np.float32).T)
    common["gnw_rep"] = np.ascontiguousarray(
        np.broadcast_to(np.asarray(i["L0_attn_gn_w"], np.float32)[None, :], (P, VD)))
    common["gnb_rep"] = np.ascontiguousarray(
        np.broadcast_to(np.asarray(i["L0_attn_gn_b"], np.float32)[None, :], (P, VD)))
    common["gb_rep"] = np.ascontiguousarray(
        np.broadcast_to(np.asarray(i["L0_attn_g_b"], np.float32)[None, :], (P, VD)))
    common["gbias_rep"] = np.ascontiguousarray(
        np.broadcast_to(np.asarray(i["L0_ffn_gate_bias"], np.float32)[None, :], (P, NE)))
    common["ident_f"] = np.eye(P, dtype=np.float32)

    sidx = np.arange(P)
    blk = sidx[:, None] // CH == sidx[None, :] // CH
    tri = -(blk & (sidx[:, None] <= sidx[None, :])).astype(np.float32)
    common["tri_neg"] = np.ascontiguousarray(tri)
    msk = (blk & (sidx[:, None] < sidx[None, :])).astype(np.uint32)
    common["mask_st"] = np.ascontiguousarray(msk)
    common["neg_big"] = np.full((P, NE), -1e30, np.float32)
    oh = np.zeros((NE, NE * P), np.float32)
    for e in range(NE):
        oh[e, e * P:(e + 1) * P] = 1.0
    common["onehotE"] = oh

    if _NC_CACHE is None:
        _NC_CACHE = build()
    nc = _NC_CACHE

    in_maps = []
    for c in range(8):
        s = c * TCORE
        halo = np.zeros((1, D), np.float32) if c % 4 == 0 else xf[s - 1:s]
        shard = np.ascontiguousarray(np.concatenate([halo, xf[s:s + TCORE]], 0))
        r = c % 4
        rm = np.zeros((P, 8), np.float32)
        rm[:, 0:4] = (np.arange(4)[None, :] < r)
        rm[:, 4:8] = 1.0 - rm[:, 0:4]
        m = dict(common)
        m["x"] = shard
        m["rankm"] = np.ascontiguousarray(rm)
        in_maps.append(m)

    res = run_bass_kernel_spmd(nc, in_maps, core_ids=list(range(8)))
    out = np.concatenate([np.asarray(res.results[c]["out"]) for c in range(8)], 0)
    return out.reshape(B, T, D).astype(np.float32)


# version 09 (final)
# speedup vs baseline: 1.0888x; 1.0162x over previous
from contextlib import ExitStack
import numpy as np
import concourse.bass as bass
import concourse.mybir as mybir
from concourse.bass_utils import run_bass_kernel_spmd

F32 = mybir.dt.float32
BF16 = mybir.dt.bfloat16
U32 = mybir.dt.uint32
ALU = mybir.AluOpType
ACT = mybir.ActivationFunctionType
AX = mybir.AxisListType.X

DEBUG_ATTN = False
P = 128
D = 1024
FC = 8
KD, VD, HEADS, DK, DV = 512, 1024, 4, 128, 256
INTER, NE = 512, 16
TCORE = 2048
W = 2049
CH = 64
NCH = TCORE // CH
NG = TCORE // P
EPS = 1e-5


class _Rec:
    def __init__(self, eng):
        object.__setattr__(self, "_e", eng)
        object.__setattr__(self, "last", None)

    def __getattr__(self, n):
        a = getattr(self._e, n)
        if callable(a):
            def w(*args, **kw):
                r = a(*args, **kw)
                if hasattr(r, "then_inc") and n not in ("nop", "wait_ge", "drain"):
                    object.__setattr__(self, "last", r)
                return r
            return w
        return a


def build(phases=("p1", "ffn1", "ln2", "attn", "ln3", "ffn2")):
    nc = bass.Bass("TRN2", target_bir_lowering=False)

    x_in = nc.dram_tensor("x", [W, D], F32, kind="ExternalInput").ap()
    out_t = nc.dram_tensor("out", [TCORE, D], F32, kind="ExternalOutput").ap()

    def din(name, shape, dt=F32):
        return nc.dram_tensor(name, list(shape), dt, kind="ExternalInput").ap()

    gate_wT = din("gate_wT", (D, NE))
    w1T = din("w1T", (NE, D, INTER))
    w3T = din("w3T", (NE, D, INTER))
    w2T = din("w2T", (NE, INTER, D))
    ws1T = din("ws1T", (D, INTER))
    ws3T = din("ws3T", (D, INTER))
    ws2T = din("ws2T", (INTER, D))
    xp_W1T = din("xp_W1T", (D, 160))
    xp_W2p = din("xp_W2p", (P, D))
    xp_W2g = din("xp_W2g", (32, D))
    r_WT = din("r_WT", (D, KD))
    k_WT = din("k_WT", (D, KD))
    v_WT = din("v_WT", (D, VD))
    w_AT = din("w_AT", (D, 64))
    w_BT = din("w_BT", (64, KD))
    g_AT = din("g_AT", (D, 64))
    g_BT = din("g_BT", (64, VD))
    o_WT = din("o_WT", (VD, D))
    xp_mu_p = din("xp_mu_p", (P, FC))
    xbias_p = din("xbias_p", (P, 5 * FC))
    wb_p = din("wb_p", (P, HEADS))
    bonus_p = din("bonus_p", (P, HEADS))
    gnw_rep = din("gnw_rep", (P, VD))
    gnb_rep = din("gnb_rep", (P, VD))
    gb_rep = din("gb_rep", (P, VD))
    gbias_rep = din("gbias_rep", (P, NE))
    ident_f = din("ident_f", (P, P))
    tri_neg = din("tri_neg", (P, P))
    neg_big = din("neg_big", (P, NE))
    rankm = din("rankm", (P, 8))
    mask_st = din("mask_st", (P, P), U32)
    onehotE = din("onehotE", (NE, NE * P))

    if DEBUG_ATTN:
        dbg_o = nc.dram_tensor("dbg_o", [P, NG * VD], F32, kind="ExternalOutput").ap()
        dbg_r = nc.dram_tensor("dbg_r", [P, HEADS * TCORE], F32, kind="ExternalOutput").ap()
        dbg_k = nc.dram_tensor("dbg_k", [P, HEADS * TCORE], F32, kind="ExternalOutput").ap()
        dbg_t2w = nc.dram_tensor("dbg_t2w", [64, TCORE], F32, kind="ExternalOutput").ap()
        dbg_t2g = nc.dram_tensor("dbg_t2g", [64, TCORE], F32, kind="ExternalOutput").ap()
        dbg_v = nc.dram_tensor("dbg_v", [FC, P, TCORE], F32, kind="ExternalOutput").ap()
        dbg_xl = nc.dram_tensor("dbg_xl", [P, TCORE], F32, kind="ExternalOutput").ap()
        dbg_qt = nc.dram_tensor("dbg_qt", [P, TCORE], F32, kind="ExternalOutput").ap()
        dbg_ktf = nc.dram_tensor("dbg_ktf", [P, 1024], F32, kind="ExternalOutput").ap()
        dbg_A = nc.dram_tensor("dbg_A", [P, DV], F32, kind="ExternalOutput").ap()
        dbg_mid = nc.dram_tensor("dbg_mid", [P, NCH], F32, kind="ExternalOutput").ap()
        dbg_cwc = nc.dram_tensor("dbg_cwc", [P, NCH], F32, kind="ExternalOutput").ap()
        dbg_logf = nc.dram_tensor("dbg_logf", [P, NCH], F32, kind="ExternalOutput").ap()
        dbg_dtk = nc.dram_tensor("dbg_dtk", [P, NG], F32, kind="ExternalOutput").ap()
        dbg_ktok = nc.dram_tensor("dbg_ktok", [P, 8 * P], F32, kind="ExternalOutput").ap()
        dbg_vtok = nc.dram_tensor("dbg_vtok", [P, 8 * DV], F32, kind="ExternalOutput").ap()
    cc_in = nc.dram_tensor("cc_in", [HEADS, P, DV + 1], F32)
    cc_out = nc.dram_tensor("cc_out", [HEADS, 4, P, DV + 1], F32)
    v_spill = nc.dram_tensor("v_spill", [FC, P, TCORE], BF16)

    ctx = ExitStack()
    with ctx:
        E = ctx.enter_context
        semnames = ["dx", "dxa", "dxb", "dw", "dwA", "dwB", "dw2A", "dw2B", "daA", "daB", "da", "mm", "vx", "sx", "gx", "cc", "st"]
        sems = {n: E(nc.semaphore(n)) for n in semnames}
        cnt = {n: 0 for n in semnames}

        def inc(name, k=1):
            cnt[name] += k
            return sems[name]

        def wt(eng, name, val=None):
            eng.wait_ge(sems[name], cnt[name] if val is None else val)

        def fire(eng, name):
            eng.drain()
            eng.last.then_inc(inc(name), 1)

        def dmago(eng, name, out, in_, **kw):
            eng.dma_start(out=out, in_=in_, **kw).then_inc(inc(name, 16), 16)

        xres = E(nc.sbuf_tensor("xres", [P, FC, W], F32))
        sb_idf = E(nc.sbuf_tensor("sb_idf", [P, P], F32))
        sb_idb = E(nc.sbuf_tensor("sb_idb", [P, P], BF16))
        sb_tri = E(nc.sbuf_tensor("sb_tri", [P, P], F32))
        sb_mask = E(nc.sbuf_tensor("sb_mask", [P, P], U32))
        sb_zero = E(nc.sbuf_tensor("sb_zero", [P, P], BF16))
        sb_ones1 = E(nc.sbuf_tensor("sb_ones1", [1, P], F32))
        sb_onescol = E(nc.sbuf_tensor("sb_onescol", [P, 1], F32))
        sb_gatew = E(nc.sbuf_tensor("sb_gatew", [P, FC, NE], BF16))
        sb_gbias = E(nc.sbuf_tensor("sb_gbias", [P, NE], F32))
        sb_negbig = E(nc.sbuf_tensor("sb_negbig", [P, NE], F32))
        sb_xpmu = E(nc.sbuf_tensor("sb_xpmu", [P, FC], F32))
        sb_xbias = E(nc.sbuf_tensor("sb_xbias", [P, 5 * FC], F32))
        sb_wb = E(nc.sbuf_tensor("sb_wb", [P, HEADS], F32))
        sb_u = E(nc.sbuf_tensor("sb_u", [P, HEADS], F32))
        sb_rank = E(nc.sbuf_tensor("sb_rank", [P, 8], F32))
        sb_eps = E(nc.sbuf_tensor("sb_eps", [P, 1], F32))
        sb_nh = E(nc.sbuf_tensor("sb_nh", [P, 1], F32))
        sb_half = E(nc.sbuf_tensor("sb_half", [P, 1], F32))

        block = E(nc.Block())

        def _wrap(mk):
            def runner(f):
                mk(lambda eng: f(_Rec(eng)))
            return runner
        TE = _wrap(block.tensor)
        VE = _wrap(block.vector)
        SE = _wrap(block.scalar)
        GE = _wrap(block.gpsimd)
        SY = _wrap(block.sync)

        def _(s):
            s.nop()
        SE(_)

        def _(sy):
            for dst, src in [(sb_idf, ident_f), (sb_tri, tri_neg), (sb_mask, mask_st),
                             (sb_gbias, gbias_rep), (sb_negbig, neg_big),
                             (sb_xpmu, xp_mu_p), (sb_xbias, xbias_p), (sb_wb, wb_p),
                             (sb_u, bonus_p), (sb_rank, rankm)]:
                dmago(sy, "dx", dst[:, :], src[:, :])
        SY(_)

        def _(g):
            dmago(g, "dw", sb_idb[:, :], ident_f[:, :])
            dmago(g, "dw", sb_gatew.ap()[:, :, :],
                  gate_wT.rearrange("(a p) b -> p a b", p=P))
        GE(_)

        def _(v):
            v.memset(sb_ones1[:, :], 1.0)
            v.memset(sb_onescol[:, :], 1.0)
            v.memset(sb_zero[:, :], 0.0)
            v.memset(sb_eps[:, :], EPS)
            v.memset(sb_nh[:, :], -0.5)
            v.memset(sb_half[:, :], 0.5)
            fire(v, "vx")
        VE(_)

        def join_all():
            for mk in (TE, VE, SE, GE, SY):
                def f(eng):
                    for s in semnames:
                        if s != "st":
                            wt(eng, s)
                mk(f)

        def emit_load_ln1(xln):
            with ExitStack() as stk1:
                tok0 = stk1.enter_context(nc.sbuf_tensor("p1_tok0", [P, D], F32))
                tok1 = stk1.enter_context(nc.sbuf_tensor("p1_tok1", [P, D], F32))
                sq_t = stk1.enter_context(nc.sbuf_tensor("p1_sq", [P, D], F32))
                stat = stk1.enter_context(nc.sbuf_tensor("p1_stat", [P, 6], F32))
                ln_t = stk1.enter_context(nc.sbuf_tensor("p1_ln", [P, D], BF16))
                tp0 = stk1.enter_context(nc.psum_tensor("p1_ps0", [P, 512], F32))
                tp1 = stk1.enter_context(nc.psum_tensor("p1_ps1", [P, 1024], BF16))

                toks = [tok0, tok1]
                dx0 = cnt["dx"]
                vx_done = {}

                for tt in range(17):
                    def f(sy, tt=tt):
                        if tt >= 2:
                            wt(sy, "vx", vx_done[tt - 2])
                        par = "dxa" if tt % 2 == 0 else "dxb"
                        if tt < 16:
                            dmago(sy, par, toks[tt % 2][:, :],
                                  x_in[1 + tt * P: 1 + (tt + 1) * P, :])
                        else:
                            dmago(sy, "dxa", toks[0][0:1, :], x_in[0:1, :])
                    SY(f)

                    halo = tt == 16
                    rows = 1 if halo else P
                    buf = toks[0] if halo else toks[tt % 2]
                    col0 = 0 if halo else 1 + tt * P

                    def f(s, tt=tt, buf=buf, rows=rows):
                        par = "dxa" if tt % 2 == 0 else "dxb"
                        s.wait_ge(sems[par], 16 * (tt // 2 + 1))
                        s.activation(out=sq_t[0:rows, :], in_=buf[0:rows, :],
                                     func=ACT.Square)
                        fire(s, "sx")
                    SE(f)

                    def f(v, tt=tt, buf=buf, rows=rows):
                        par = "dxa" if tt % 2 == 0 else "dxb"
                        v.wait_ge(sems[par], 16 * (tt // 2 + 1))
                        v.tensor_reduce(out=stat[0:rows, 0:1], in_=buf[0:rows, :],
                                        axis=AX, op=ALU.add)
                        wt(v, "sx")
                        v.tensor_reduce(out=stat[0:rows, 1:2], in_=sq_t[0:rows, :],
                                        axis=AX, op=ALU.add)
                        v.drain()
                        v.tensor_scalar(out=stat[0:rows, 2:3], in0=stat[0:rows, 0:1],
                                        scalar1=1.0 / D, scalar2=None, op0=ALU.mult)
                        v.tensor_scalar(out=stat[0:rows, 3:4], in0=stat[0:rows, 1:2],
                                        scalar1=1.0 / D, scalar2=None, op0=ALU.mult)
                        v.drain()
                        v.tensor_tensor(out=stat[0:rows, 4:5], in0=stat[0:rows, 2:3],
                                        in1=stat[0:rows, 2:3], op=ALU.mult)
                        v.drain()
                        v.tensor_tensor(out=stat[0:rows, 4:5], in0=stat[0:rows, 3:4],
                                        in1=stat[0:rows, 4:5], op=ALU.subtract)
                        fire(v, "vx")
                    VE(f)

                    def f(s, rows=rows):
                        wt(s, "vx")
                        s.activation(out=stat[0:rows, 5:6], in_=stat[0:rows, 4:5],
                                     func=ACT.Ln, bias=sb_eps[0:rows, :])
                        s.drain()
                        s.activation(out=stat[0:rows, 5:6], in_=stat[0:rows, 5:6],
                                     func=ACT.Exp, scale=sb_nh[0:rows, :])
                        fire(s, "sx")
                    SE(f)

                    def f(v, buf=buf, rows=rows):
                        wt(v, "sx")
                        v.tensor_scalar(out=ln_t[0:rows, :], in0=buf[0:rows, :],
                                        scalar1=stat[0:rows, 2:3],
                                        scalar2=stat[0:rows, 5:6],
                                        op0=ALU.subtract, op1=ALU.mult)
                        fire(v, "vx")
                    VE(f)

                    for fc in range(FC):
                        def f(t, fc=fc, buf=buf, rows=rows):
                            wt(t, "vx")
                            wt(t, "dx")
                            t.transpose(tp0[:, 0:rows], buf[0:rows, fc * P:(fc + 1) * P],
                                        sb_idf[0:rows, 0:rows])
                            wt(t, "dw")
                            t.transpose(tp1[:, 0:rows], ln_t[0:rows, fc * P:(fc + 1) * P],
                                        sb_idb[0:rows, 0:rows])
                            t.last.then_inc(inc("mm"), 1)
                        TE(f)

                        def f(v, fc=fc, col0=col0, rows=rows, tt=tt,
                              last=(fc == FC - 1)):
                            wt(v, "mm")
                            v.tensor_copy(xres[:, fc, col0:col0 + rows], tp0[:, 0:rows])
                            v.tensor_copy(xln[:, fc, col0:col0 + rows], tp1[:, 0:rows])
                            fire(v, "vx")
                            if last:
                                vx_done[tt] = cnt["vx"]
                        VE(f)

                        def f(t):
                            wt(t, "vx")
                        TE(f)

            join_all()

        def emit_ln(xln, c0, c1, tag=""):
            with ExitStack() as stk2:
                sqb0 = stk2.enter_context(nc.sbuf_tensor(tag + "ln_sq0", [P, 512], F32))
                sqb1 = stk2.enter_context(nc.sbuf_tensor(tag + "ln_sq1", [P, 512], F32))
                sums = stk2.enter_context(nc.sbuf_tensor(tag + "ln_sums", [1, W], F32))
                sqs = stk2.enter_context(nc.sbuf_tensor(tag + "ln_sqs", [1, W], F32))
                s_str = stk2.enter_context(nc.sbuf_tensor(tag + "ln_s", [1, W], F32))
                b_str = stk2.enter_context(nc.sbuf_tensor(tag + "ln_b", [1, W], F32))
                srep = stk2.enter_context(nc.sbuf_tensor(tag + "ln_srep", [P, W], F32))
                brep = stk2.enter_context(nc.sbuf_tensor(tag + "ln_brep", [P, W], F32))
                psA = stk2.enter_context(nc.psum_tensor(tag + "ln_pA", [P, 512], F32))
                psB = stk2.enter_context(nc.psum_tensor(tag + "ln_pB", [P, 512], F32))
                psR = stk2.enter_context(nc.psum_tensor(tag + "ln_pR", [P, 512], F32))

                sqb = [sqb0, sqb1]
                slabs = []
                c = c0
                while c < c1:
                    wdt = min(512, c1 - c)
                    slabs.append((c, wdt))
                    c += wdt

                sx_sq = {}
                for (c, wdt) in slabs:
                    for fc in range(FC):
                        def f(s, c=c, wdt=wdt, fc=fc):
                            if fc >= 2 or c != slabs[0][0]:
                                wt(s, "mm")
                            s.activation(out=sqb[fc % 2][:, 0:wdt],
                                         in_=xres[:, fc, c:c + wdt], func=ACT.Square)
                            fire(s, "sx")
                            sx_sq[(c, fc)] = cnt["sx"]
                        SE(f)

                        def f(t, c=c, wdt=wdt, fc=fc):
                            wt(t, "sx", sx_sq[(c, fc)])
                            t.matmul(psA[0:1, 0:wdt], sb_onescol[:, :],
                                     xres[:, fc, c:c + wdt],
                                     start=(fc == 0), stop=(fc == FC - 1))
                            t.matmul(psB[0:1, 0:wdt], sb_onescol[:, :],
                                     sqb[fc % 2][:, 0:wdt],
                                     start=(fc == 0), stop=(fc == FC - 1))
                            t.last.then_inc(inc("mm"), 1)
                        TE(f)

                    def f(v, c=c, wdt=wdt):
                        wt(v, "mm")
                        v.tensor_copy(sums[0:1, c:c + wdt], psA[0:1, 0:wdt])
                        v.tensor_copy(sqs[0:1, c:c + wdt], psB[0:1, 0:wdt])
                        fire(v, "vx")
                    VE(f)

                    def f(t):
                        wt(t, "vx")
                    TE(f)

                def f(v):
                    wt(v, "mm")
                    v.tensor_scalar(out=sums[0:1, c0:c1], in0=sums[0:1, c0:c1],
                                    scalar1=1.0 / D, scalar2=None, op0=ALU.mult)
                    v.tensor_scalar(out=sqs[0:1, c0:c1], in0=sqs[0:1, c0:c1],
                                    scalar1=1.0 / D, scalar2=None, op0=ALU.mult)
                    v.drain()
                    v.tensor_tensor(out=s_str[0:1, c0:c1], in0=sums[0:1, c0:c1],
                                    in1=sums[0:1, c0:c1], op=ALU.mult)
                    v.drain()
                    v.tensor_tensor(out=s_str[0:1, c0:c1], in0=sqs[0:1, c0:c1],
                                    in1=s_str[0:1, c0:c1], op=ALU.subtract)
                    fire(v, "vx")
                VE(f)

                def f(s):
                    wt(s, "vx")
                    s.activation(out=s_str[0:1, c0:c1], in_=s_str[0:1, c0:c1],
                                 func=ACT.Ln, bias=sb_eps[0:1, :])
                    s.drain()
                    s.activation(out=s_str[0:1, c0:c1], in_=s_str[0:1, c0:c1],
                                 func=ACT.Exp, scale=sb_nh[0:1, :])
                    fire(s, "sx")
                SE(f)

                def f(v):
                    wt(v, "sx")
                    v.scalar_tensor_tensor(out=b_str[0:1, c0:c1], in0=sums[0:1, c0:c1],
                                           scalar=-1.0, in1=s_str[0:1, c0:c1],
                                           op0=ALU.mult, op1=ALU.mult)
                    fire(v, "vx")
                VE(f)

                for (strip, rep) in [(s_str, srep), (b_str, brep)]:
                    for (c, wdt) in slabs:
                        def f(t, strip=strip, c=c, wdt=wdt):
                            wt(t, "vx")
                            t.matmul(psR[:, 0:wdt], sb_ones1[:, :],
                                     strip[0:1, c:c + wdt], start=True, stop=True)
                            t.last.then_inc(inc("mm"), 1)
                        TE(f)

                        def f(v, rep=rep, c=c, wdt=wdt):
                            wt(v, "mm")
                            v.tensor_copy(rep[:, c:c + wdt], psR[:, 0:wdt])
                            fire(v, "vx")
                        VE(f)

                        def f(t):
                            wt(t, "vx")
                        TE(f)

                def f(v):
                    for fc in range(FC):
                        v.scalar_tensor_tensor(out=xln[:, fc, c0:c1],
                                               in0=xres[:, fc, c0:c1], scalar=1.0,
                                               in1=srep[:, c0:c1],
                                               op0=ALU.mult, op1=ALU.mult)
                        v.drain()
                        v.tensor_tensor(out=xln[:, fc, c0:c1], in0=xln[:, fc, c0:c1],
                                        in1=brep[:, c0:c1], op=ALU.add)
                        v.drain()
                    fire(v, "vx")
                VE(f)

            join_all()

        def emit_ffn(xln, tag, c0, c1):
            ntt = (c1 + P - 1) // P
            tcms = []
            c = c0
            while c < c1:
                rem = c1 - c
                wdt = min(512, rem)
                if rem == 513:
                    wdt = 256
                tcms.append((c, wdt))
                c += wdt

            with ExitStack() as stk3:
                scores = stk3.enter_context(nc.sbuf_tensor(tag + "sc", [P, ntt, NE], F32))
                rt0 = stk3.enter_context(nc.sbuf_tensor(tag + "r0", [P, ntt, NE], F32))
                rt1 = stk3.enter_context(nc.sbuf_tensor(tag + "r1", [P, ntt, NE], F32))
                rtg = stk3.enter_context(nc.sbuf_tensor(tag + "rg", [P, ntt, 4], F32))
                rtg2 = stk3.enter_context(nc.sbuf_tensor(tag + "rg2", [P, ntt, 4], F32))
                rtm = stk3.enter_context(nc.sbuf_tensor(tag + "rm", [P, ntt], F32))
                keepg = stk3.enter_context(nc.sbuf_tensor(tag + "kg", [P, ntt, 4], U32))
                comb = stk3.enter_context(nc.sbuf_tensor(tag + "cbm", [P, ntt, NE], F32))
                combE = stk3.enter_context(nc.sbuf_tensor(tag + "cbE", [NE, ntt * P], F32))
                oneh = stk3.enter_context(nc.sbuf_tensor(tag + "oneh", [NE, NE * P], F32))
                crep = stk3.enter_context(nc.sbuf_tensor(tag + "crep", [P, W], BF16))
                w1a = stk3.enter_context(nc.sbuf_tensor(tag + "w1a", [P, FC, INTER], BF16))
                w1b = stk3.enter_context(nc.sbuf_tensor(tag + "w1b", [P, FC, INTER], BF16))
                w3a = stk3.enter_context(nc.sbuf_tensor(tag + "w3a", [P, FC, INTER], BF16))
                w3b = stk3.enter_context(nc.sbuf_tensor(tag + "w3b", [P, FC, INTER], BF16))
                w2a = stk3.enter_context(nc.sbuf_tensor(tag + "w2a", [P, 4, D], BF16))
                w2b = stk3.enter_context(nc.sbuf_tensor(tag + "w2b", [P, 4, D], BF16))
                hp = stk3.enter_context(nc.sbuf_tensor(tag + "hp", [P, 4, 512], BF16))
                h3t0 = stk3.enter_context(nc.sbuf_tensor(tag + "h3t0", [P, 512], BF16))
                h3t1 = stk3.enter_context(nc.sbuf_tensor(tag + "h3t1", [P, 512], BF16))
                yt0 = stk3.enter_context(nc.sbuf_tensor(tag + "yt0", [P, 4, 256], BF16))
                yt1 = stk3.enter_context(nc.sbuf_tensor(tag + "yt1", [P, 4, 256], BF16))
                p1a = stk3.enter_context(nc.psum_tensor(tag + "p1a", [P, 512], F32))
                p1b = stk3.enter_context(nc.psum_tensor(tag + "p1b", [P, 512], F32))
                p3a = stk3.enter_context(nc.psum_tensor(tag + "p3a", [P, 512], F32))
                p3b = stk3.enter_context(nc.psum_tensor(tag + "p3b", [P, 512], F32))
                py0 = stk3.enter_context(nc.psum_tensor(tag + "py0", [P, 4, 256], F32))
                py1 = stk3.enter_context(nc.psum_tensor(tag + "py1", [P, 4, 256], F32))

                ph1 = [p1a, p1b]
                ph3 = [p3a, p3b]
                pys = [py0, py1]
                w1bufs, w3bufs, w2bufs = [w1a, w1b], [w3a, w3b], [w2a, w2b]
                h3ts = [h3t0, h3t1]
                yts = [yt0, yt1]

                def f(sy):
                    dmago(sy, "dx", oneh[:, :], onehotE[:, :])
                SY(f)

                def f(v):
                    v.memset(scores.ap().rearrange("p a e -> p (a e)"), 0.0)
                    if c0 > 0:
                        v.memset(xln[:, :, 0:c0].rearrange("p a b -> p (a b)"), 0.0)
                    fire(v, "vx")
                VE(f)

                rt_vx0 = cnt["vx"]
                sx_rt = {}
                for tt in range(ntt):
                    t0 = tt * P
                    rows = min(P, c1 - t0)

                    def f(t, tt=tt, t0=t0, rows=rows):
                        wt(t, "vx", rt_vx0)
                        if tt >= 1:
                            wt(t, "sx", sx_rt[tt - 1])
                        for fc in range(FC):
                            t.matmul(p1a[0:rows, 0:NE], xln[:, fc, t0:t0 + rows],
                                     sb_gatew[:, fc, :],
                                     start=(fc == 0), stop=(fc == FC - 1))
                        t.last.then_inc(inc("mm"), 1)
                    TE(f)

                    def f(s, tt=tt, rows=rows):
                        wt(s, "mm")
                        wt(s, "vx", rt_vx0)
                        s.activation(out=scores[0:rows, tt, :], in_=p1a[0:rows, 0:NE],
                                     func=ACT.Sigmoid)
                        fire(s, "sx")
                        sx_rt[tt] = cnt["sx"]
                    SE(f)

                def f(v):
                    wt(v, "sx")
                    v.tensor_tensor(out=rt0[:, :, :], in0=scores.ap(),
                                    in1=sb_gbias[:, :].unsqueeze(1)
                                    .to_broadcast([P, ntt, NE]), op=ALU.add)
                    v.drain()
                    v.tensor_reduce(out=rtg.ap().rearrange("p a g -> p (a g)"),
                                    in_=rt0.ap().rearrange("p a (g e) -> p (a g) e", e=4),
                                    axis=AX, op=ALU.max)
                    v.drain()
                    v.tensor_reduce(out=rtm[:, :], in_=rtg[:, :, :], axis=AX, op=ALU.max)
                    v.drain()
                    v.tensor_tensor(out=rtg2[:, :, :], in0=rtg[:, :, :],
                                    in1=rtm.ap().unsqueeze(2).to_broadcast([P, ntt, 4]),
                                    op=ALU.is_ge)
                    v.drain()
                    v.scalar_tensor_tensor(out=rtg2[:, :, :], in0=rtg2[:, :, :],
                                           scalar=-1e30, in1=rtg[:, :, :],
                                           op0=ALU.mult, op1=ALU.add)
                    v.drain()
                    v.tensor_reduce(out=rtm[:, :], in_=rtg2[:, :, :], axis=AX, op=ALU.max)
                    v.drain()
                    v.tensor_tensor(out=keepg[:, :, :], in0=rtg[:, :, :],
                                    in1=rtm.ap().unsqueeze(2).to_broadcast([P, ntt, 4]),
                                    op=ALU.is_ge)
                    v.drain()
                    v.tensor_scalar(out=rtg2[:, :, :], in0=keepg[:, :, :],
                                    scalar1=1e30, scalar2=-1e30,
                                    op0=ALU.mult, op1=ALU.add)
                    v.drain()
                    v.tensor_tensor(out=rt1.ap().rearrange("p a (g e) -> p a g e", e=4),
                                    in0=rt0.ap().rearrange("p a (g e) -> p a g e", e=4),
                                    in1=rtg2.ap().unsqueeze(3)
                                    .to_broadcast([P, ntt, 4, 4]),
                                    op=ALU.add)
                    v.drain()
                    v.tensor_copy(rt0.ap().rearrange("p a e -> p (a e)"),
                                  rt1.ap().rearrange("p a e -> p (a e)"))
                    v.drain()
                    for _i in range(3):
                        v.tensor_reduce(out=rtm[:, :], in_=rt0[:, :, :], axis=AX,
                                        op=ALU.max)
                        v.drain()
                        v.tensor_tensor(out=comb[:, :, :], in0=rt0[:, :, :],
                                        in1=rtm.ap().unsqueeze(2)
                                        .to_broadcast([P, ntt, NE]), op=ALU.is_ge)
                        v.drain()
                        v.scalar_tensor_tensor(out=rt0[:, :, :], in0=comb[:, :, :],
                                               scalar=-1e30, in1=rt0[:, :, :],
                                               op0=ALU.mult, op1=ALU.add)
                        v.drain()
                    v.tensor_reduce(out=rtm[:, :], in_=rt0[:, :, :], axis=AX, op=ALU.max)
                    v.drain()
                    v.tensor_tensor(out=comb[:, :, :], in0=rt1[:, :, :],
                                    in1=rtm.ap().unsqueeze(2).to_broadcast([P, ntt, NE]),
                                    op=ALU.is_ge)
                    v.drain()
                    v.tensor_tensor(out=comb[:, :, :], in0=comb[:, :, :],
                                    in1=scores.ap(), op=ALU.mult)
                    fire(v, "vx")
                VE(f)

                for tt in range(ntt):
                    rows = min(P, c1 - tt * P)

                    def f(t, tt=tt, rows=rows):
                        wt(t, "vx")
                        t.transpose(p1a[0:NE, 0:rows], comb[0:rows, tt, :],
                                    sb_idf[0:rows, 0:rows])
                        t.last.then_inc(inc("mm"), 1)
                    TE(f)

                    def f(v, tt=tt, rows=rows):
                        wt(v, "mm")
                        v.tensor_copy(combE[:, tt * P:tt * P + rows], p1a[0:NE, 0:rows])
                        fire(v, "vx")
                    VE(f)

                    def f(t):
                        wt(t, "vx")
                    TE(f)

                def load_w13(g, e):
                    b = e % 2
                    sem = "dwA" if b == 0 else "dwB"
                    s1 = w1T[e] if e < NE else ws1T
                    s3 = w3T[e] if e < NE else ws3T
                    dmago(g, sem, w1bufs[b].ap()[:, :, :],
                          s1.rearrange("(a p) b -> p a b", p=P))
                    dmago(g, sem, w3bufs[b].ap()[:, :, :],
                          s3.rearrange("(a p) b -> p a b", p=P))
                    return cnt[sem]

                def load_w2(g, e):
                    b = e % 2
                    sem = "dw2A" if b == 0 else "dw2B"
                    s2 = w2T[e] if e < NE else ws2T
                    dmago(g, sem, w2bufs[b].ap()[:, :, :],
                          s2.rearrange("(a p) b -> p a b", p=P))
                    return cnt[sem]

                dw13, dw2 = {}, {}
                mm_h, mm_y = {}, {}
                ev_i = {}
                ysx = {}
                yda = {}
                crep_vx = {}

                def f(g):
                    for e0 in (0, 1):
                        dw13[e0] = load_w13(g, e0)
                        dw2[e0] = load_w2(g, e0)
                GE(f)

                NEXP = NE + 1
                for e in range(NEXP):
                    if e < NE:
                        for ci, (c, wdt) in enumerate(tcms):
                            def f(t, e=e, c=c, wdt=wdt, ci=ci):
                                if ci == 0:
                                    wt(t, "dx")
                                    if e >= 1:
                                        wt(t, "vx",
                                           ev_i[(e - 1, len(tcms) - 1, 3)])
                                else:
                                    wt(t, "vx")
                                t.matmul(p1a[:, 0:wdt],
                                         oneh[:, e * P:(e + 1) * P],
                                         combE[0:NE, c:c + wdt],
                                         start=True, stop=True)
                                t.last.then_inc(inc("mm"), 1)
                            TE(f)

                            def f(v, c=c, wdt=wdt):
                                wt(v, "mm")
                                v.tensor_copy(crep[:, c:c + wdt], p1a[:, 0:wdt])
                                fire(v, "vx")
                            VE(f)
                        crep_vx[e] = cnt["vx"]

                        def f(t):
                            wt(t, "vx", crep_vx[e])
                        TE(f)

                    for ti, (c, wdt) in enumerate(tcms):
                        mm_after_h = {}

                        def emit_evac(i, mmtarget, e=e, ti=ti, c=c, wdt=wdt):
                            pb = i % 2

                            def f(s, i=i, pb=pb, wdt=wdt, mmtarget=mmtarget):
                                wt(s, "mm", mmtarget)
                                s.activation(out=hp[:, i, 0:wdt], in_=ph1[pb][:, 0:wdt],
                                             func=ACT.Silu)
                                fire(s, "sx")
                            SE(f)

                            def f(v, i=i, pb=pb, c=c, wdt=wdt, e=e, ti=ti,
                                  mmtarget=mmtarget):
                                wt(v, "mm", mmtarget)
                                wt(v, "sx")
                                v.scalar_tensor_tensor(out=hp[:, i, 0:wdt],
                                                       in0=ph3[pb][:, 0:wdt],
                                                       scalar=1.0, in1=hp[:, i, 0:wdt],
                                                       op0=ALU.mult, op1=ALU.mult)
                                v.drain()
                                if e < NE:
                                    v.tensor_tensor(out=hp[:, i, 0:wdt],
                                                    in0=hp[:, i, 0:wdt],
                                                    in1=crep[:, c:c + wdt], op=ALU.mult)
                                    v.drain()
                                v.last.then_inc(inc("vx"), 1)
                                ev_i[(e, ti, i)] = cnt["vx"]
                            VE(f)

                        for i in range(4):
                            def f(t, i=i, e=e, ti=ti, c=c, wdt=wdt):
                                if i == 0:
                                    t.wait_ge(sems["dwA" if e % 2 == 0 else "dwB"],
                                              dw13[e])
                                    if e < NE:
                                        wt(t, "vx", crep_vx[e])
                                    if ti >= 1:
                                        wt(t, "vx", ev_i[(e, ti - 1, 3)])
                                    elif e >= 1:
                                        wt(t, "vx", ev_i[(e - 1, len(tcms) - 1, 3)])
                                if i >= 2:
                                    wt(t, "vx", ev_i[(e, ti, i - 2)])
                                pb = i % 2
                                for kc in range(FC):
                                    t.matmul(ph1[pb][:, 0:wdt],
                                             w1bufs[e % 2][:, kc, i * P:(i + 1) * P],
                                             xln[:, kc, c:c + wdt],
                                             start=(kc == 0), stop=(kc == FC - 1))
                                for kc in range(FC):
                                    t.matmul(ph3[pb][:, 0:wdt],
                                             w3bufs[e % 2][:, kc, i * P:(i + 1) * P],
                                             xln[:, kc, c:c + wdt],
                                             start=(kc == 0), stop=(kc == FC - 1))
                                t.last.then_inc(inc("mm"), 1)
                                mm_after_h[i] = cnt["mm"]
                            TE(f)
                            if i >= 1:
                                emit_evac(i - 1, mm_after_h[i - 1])
                        emit_evac(3, mm_after_h[3])
                        mm_h[e] = cnt["mm"]

                        if wdt > 256:
                            a = (wdt + 1) // 2
                            hws = [(0, a), (a, wdt - a)]
                        else:
                            hws = [(0, wdt)]
                        for hf, (h0, hw) in enumerate(hws):
                            for dp in range(2):
                                def f(t, e=e, ti=ti, hf=hf, dp=dp, h0=h0, hw=hw):
                                    if hf == 0 and dp == 0:
                                        if ti == 0:
                                            t.wait_ge(sems["dw2A" if e % 2 == 0
                                                           else "dw2B"], dw2[e])
                                        wt(t, "vx", ev_i[(e, ti, 3)])
                                    prev = ysx.get(dp)
                                    if prev is not None:
                                        wt(t, "vx", prev)
                                    for dci in range(4):
                                        dc = 4 * dp + dci
                                        for i in range(4):
                                            t.matmul(pys[dp][:, dci, 0:hw],
                                                     w2bufs[e % 2][:, i,
                                                                   dc * P:(dc + 1) * P],
                                                     hp[:, i, h0:h0 + hw],
                                                     start=(i == 0), stop=(i == 3))
                                    t.last.then_inc(inc("mm"), 1)
                                TE(f)
                                mmy = cnt["mm"]

                                def f(v, dp=dp, hw=hw, mmy=mmy):
                                    wt(v, "mm", mmy)
                                    prev_da = yda.get(dp)
                                    if prev_da is not None:
                                        v.wait_ge(sems["daA" if dp == 0 else "daB"],
                                                  prev_da)
                                    v.tensor_scalar(out=yts[dp].ap()[:, :, 0:hw],
                                                    in0=pys[dp].ap()[:, :, 0:hw],
                                                    scalar1=0.5, scalar2=None,
                                                    op0=ALU.mult)
                                    fire(v, "vx")
                                VE(f)
                                ysx[dp] = cnt["vx"]
                                sx_now = cnt["vx"]

                                def f(g, c=c, h0=h0, hw=hw, dp=dp, sx_now=sx_now):
                                    wt(g, "vx", sx_now)
                                    dmago(g, "daA" if dp == 0 else "daB",
                                          xres[:, 4 * dp:4 * dp + 4,
                                               c + h0:c + h0 + hw],
                                          yts[dp].ap()[:, :, 0:hw],
                                          accum_op=ALU.add)
                                GE(f)
                                yda[dp] = cnt["daA" if dp == 0 else "daB"]
                        mm_y[e] = cnt["mm"]

                    if e + 2 < NEXP:
                        def f(g, e=e):
                            g.wait_ge(sems["mm"], mm_h[e])
                            dw13[e + 2] = load_w13(g, e + 2)
                            g.wait_ge(sems["mm"], mm_y[e])
                            dw2[e + 2] = load_w2(g, e + 2)
                        GE(f)

            join_all()

        def emit_attn(xln):
            with ExitStack() as stk4:
                t2g = stk4.enter_context(nc.sbuf_tensor("a_t2g", [64, TCORE], BF16))
                o_sb = stk4.enter_context(nc.sbuf_tensor("a_o", [P, NG, VD], BF16))
                ab_stk = ExitStack()
                r_sb = ab_stk.enter_context(nc.sbuf_tensor("a_r", [P, HEADS, TCORE], BF16))
                k_sb = ab_stk.enter_context(nc.sbuf_tensor("a_k", [P, HEADS, TCORE], BF16))
                t2w = ab_stk.enter_context(nc.sbuf_tensor("a_t2w", [64, TCORE], BF16))
                swB = ab_stk.enter_context(nc.sbuf_tensor("a_wB", [64, KD], BF16))

                def f(g):
                    dmago(g, "dw", swB[:, :], w_BT[:, :])
                GE(f)

                with ExitStack() as stk5:
                    xl_a = stk5.enter_context(nc.sbuf_tensor("aa_xla", [P, TCORE], BF16))
                    xl_b = stk5.enter_context(nc.sbuf_tensor("aa_xlb", [32, TCORE], BF16))
                    xb0 = stk5.enter_context(nc.sbuf_tensor("aa_xb0", [P, 512], BF16))
                    xb1 = stk5.enter_context(nc.sbuf_tensor("aa_xb1", [P, 512], BF16))
                    dtile = stk5.enter_context(nc.sbuf_tensor("aa_dt", [P, 512], BF16))
                    vtile = stk5.enter_context(nc.sbuf_tensor("aa_vt", [P, 512], BF16))
                    wstr = stk5.enter_context(nc.sbuf_tensor("aa_wst", [P, FC, 512], BF16))
                    sxpw1 = stk5.enter_context(nc.sbuf_tensor("aa_xpw1", [P, FC, 160], BF16))
                    sxpw2p = stk5.enter_context(nc.sbuf_tensor("aa_xpw2p", [P, D], BF16))
                    sxpw2g = stk5.enter_context(nc.sbuf_tensor("aa_xpw2g", [32, D], BF16))
                    swA = stk5.enter_context(nc.sbuf_tensor("aa_wA", [P, FC, 64], BF16))
                    sgA = stk5.enter_context(nc.sbuf_tensor("aa_gA", [P, FC, 64], BF16))
                    pj = stk5.enter_context(nc.psum_tensor("aa_pj", [P, 512], F32))
                    pjs = stk5.enter_context(nc.psum_tensor("aa_pjs", [P, 512], F32))
                    pr0 = stk5.enter_context(nc.psum_tensor("aa_pr0", [P, 512], F32))
                    pr1 = stk5.enter_context(nc.psum_tensor("aa_pr1", [P, 512], F32))
                    pr2 = stk5.enter_context(nc.psum_tensor("aa_pr2", [P, 512], F32))
                    pr3 = stk5.enter_context(nc.psum_tensor("aa_pr3", [P, 512], F32))

                    prs = [pr0, pr1, pr2, pr3]
                    xbs = [xb0, xb1]

                    def f(g):
                        dmago(g, "dw", sxpw1.ap()[:, :, :],
                              xp_W1T.rearrange("(a p) b -> p a b", p=P))
                        dmago(g, "dw", sxpw2p[:, :], xp_W2p[:, :])
                        dmago(g, "dw", sxpw2g[:, :], xp_W2g[:, :])
                        dmago(g, "dw", swA.ap()[:, :, :],
                              w_AT.rearrange("(a p) b -> p a b", p=P))
                        dmago(g, "dw", sgA.ap()[:, :, :],
                              g_AT.rearrange("(a p) b -> p a b", p=P))
                        dmago(g, "dw", wstr.ap()[:, :, :],
                              r_WT.rearrange("(a p) b -> p a b", p=P))
                    GE(f)
                    dw_r = cnt["dw"]

                    vx_xb = {}
                    for ts in range(4):
                        t0 = ts * 512
                        for fc in range(FC):
                            def f(v, ts=ts, t0=t0, fc=fc):
                                wt(v, "mm")
                                b = fc % 2
                                v.tensor_tensor(out=dtile[:, :],
                                                in0=xln[:, fc, t0:t0 + 512],
                                                in1=xln[:, fc, 1 + t0:1 + t0 + 512],
                                                op=ALU.subtract)
                                v.drain()
                                v.scalar_tensor_tensor(out=xbs[b][:, :], in0=dtile[:, :],
                                                       scalar=sb_xpmu[:, fc:fc + 1],
                                                       in1=xln[:, fc, 1 + t0:1 + t0 + 512],
                                                       op0=ALU.mult, op1=ALU.add)
                                fire(v, "vx")
                                vx_xb[(ts, fc)] = cnt["vx"]
                            VE(f)

                            def f(t, ts=ts, fc=fc):
                                wt(t, "vx", vx_xb[(ts, fc)])
                                if ts == 0 and fc == 0:
                                    wt(t, "dw")
                                b = fc % 2
                                t.matmul(pj[:, :], sxpw1[:, fc, 0:P], xbs[b][:, :],
                                         start=(fc == 0), stop=(fc == FC - 1))
                                t.matmul(pjs[0:32, :], sxpw1[:, fc, P:160], xbs[b][:, :],
                                         start=(fc == 0), stop=(fc == FC - 1))
                                t.last.then_inc(inc("mm"), 1)
                            TE(f)

                        def f(s, ts=ts, t0=t0):
                            wt(s, "mm")
                            s.activation(out=xl_a[:, t0:t0 + 512], in_=pj[:, :],
                                         func=ACT.Tanh)
                            s.drain()
                            s.activation(out=xl_b[:, t0:t0 + 512], in_=pjs[0:32, :],
                                         func=ACT.Tanh)
                            fire(s, "sx")
                        SE(f)

                        def f(t):
                            wt(t, "sx")
                        TE(f)

                    def emit_branch(n, kind, dest=None, dwwait=None, spill_base=None):
                        pjp = [pj, pjs]
                        dts = [dtile, vtile]

                        def mu_mm(t, n, fc, t0):
                            if n < 4:
                                t.matmul(pjp[fc % 2][:, :],
                                         sxpw2p[32 * n:32 * n + 32,
                                                fc * P:(fc + 1) * P],
                                         xl_a[32 * n:32 * n + 32, t0:t0 + 512],
                                         start=True, stop=True,
                                         tile_position=(32 * n, 0))
                            else:
                                t.matmul(pjp[fc % 2][:, :],
                                         sxpw2g[:, fc * P:(fc + 1) * P],
                                         xl_b[:, t0:t0 + 512],
                                         start=True, stop=True)
                            t.last.then_inc(inc("mm"), 1)

                        def proj_mm(t, kind, fc):
                            b = fc % 2
                            if kind in ("big4", "bigv"):
                                for hc in range(4):
                                    t.matmul(prs[hc][:, :],
                                             wstr[:, fc, hc * P:(hc + 1) * P],
                                             xbs[b][:, :],
                                             start=(fc == 0), stop=(fc == FC - 1))
                            else:
                                A = swA if kind == "rankw" else sgA
                                t.matmul(prs[0][0:64, :], A[:, fc, :], xbs[b][:, :],
                                         start=(fc == 0), stop=(fc == FC - 1))
                            t.last.then_inc(inc("mm"), 1)

                        for ts in range(4):
                            t0 = ts * 512
                            vx_b = {}
                            mm_mu = {}
                            mm_pj = {}

                            def f(t, n=n, t0=t0, dwwait=dwwait):
                                if dwwait is not None:
                                    t.wait_ge(sems["dw"], dwwait)
                                wt(t, "vx")
                                mu_mm(t, n, 0, t0)
                                mm_mu[0] = cnt["mm"]
                                mu_mm(t, n, 1, t0)
                                mm_mu[1] = cnt["mm"]
                            TE(f)

                            for fc in range(FC):
                                def f(v, n=n, ts=ts, t0=t0, fc=fc):
                                    b = fc % 2
                                    v.tensor_tensor(out=dts[b][:, :],
                                                    in0=xln[:, fc, t0:t0 + 512],
                                                    in1=xln[:, fc, 1 + t0:1 + t0 + 512],
                                                    op=ALU.subtract)
                                    wt(v, "mm", max(mm_mu[fc], mm_pj.get(fc - 2, 0)))
                                    v.drain()
                                    v.scalar_tensor_tensor(
                                        out=xbs[b][:, :], in0=pjp[fc % 2][:, :],
                                        scalar=sb_xbias[:, n * FC + fc:n * FC + fc + 1],
                                        in1=dts[b][:, :], op0=ALU.add, op1=ALU.mult)
                                    v.drain()
                                    v.tensor_tensor(out=xbs[b][:, :], in0=xbs[b][:, :],
                                                    in1=xln[:, fc, 1 + t0:1 + t0 + 512],
                                                    op=ALU.add)
                                    fire(v, "vx")
                                    vx_b[fc] = cnt["vx"]
                                VE(f)

                                def f(t, kind=kind, n=n, fc=fc, t0=t0):
                                    wt(t, "vx", vx_b[fc])
                                    if fc + 2 < FC:
                                        mu_mm(t, n, fc + 2, t0)
                                        mm_mu[fc + 2] = cnt["mm"]
                                    proj_mm(t, kind, fc)
                                    mm_pj[fc] = cnt["mm"]
                                TE(f)

                            mm_pr = cnt["mm"]
                            if kind == "big4":
                                def f(s, ts=ts, t0=t0, dest=dest, mm_pr=mm_pr):
                                    wt(s, "mm", mm_pr)
                                    for hc in range(4):
                                        s.activation(out=dest[:, hc, t0:t0 + 512],
                                                     in_=prs[hc][:, :], func=ACT.Copy)
                                    fire(s, "sx")
                                SE(f)
                            elif kind == "bigv":
                                def f(s, ts=ts, mm_pr=mm_pr):
                                    wt(s, "mm", mm_pr)
                                    wt(s, "dw")
                                    for hc in range(4):
                                        s.activation(out=vtile[:, 0:512] if False else
                                                     vtile[:, :],
                                                     in_=prs[hc][:, :], func=ACT.Copy)
                                        s.drain()
                                        s.nop().then_inc(inc("sx"), 1)
                                    fire(s, "sx")
                                SE(f)
                            else:
                                def f(s, ts=ts, t0=t0, dest=dest, mm_pr=mm_pr):
                                    wt(s, "mm", mm_pr)
                                    s.activation(out=dest[:, t0:t0 + 512],
                                                 in_=prs[0][0:64, :], func=ACT.Tanh)
                                    fire(s, "sx")
                                SE(f)

                            def f(t):
                                wt(t, "sx")
                            TE(f)

                    emit_branch(0, "big4", dest=r_sb, dwwait=dw_r)
                    emit_branch(1, "rankw", dest=t2w)

                    def f(g):
                        wt(g, "mm")
                        dmago(g, "dw", wstr.ap()[:, :, :],
                              k_WT.rearrange("(a p) b -> p a b", p=P))
                    GE(f)
                    dw_k = cnt["dw"]

                    emit_branch(2, "big4", dest=k_sb, dwwait=dw_k)
                    emit_branch(4, "rankg", dest=t2g)

                    for vp in range(2):
                        def f(g, vp=vp):
                            wt(g, "mm")
                            dmago(g, "dw", wstr.ap()[:, :, :],
                                  v_WT[:, vp * 512:(vp + 1) * 512]
                                  .rearrange("(a p) b -> p a b", p=P))
                        GE(f)
                        dw_v = cnt["dw"]

                        for ts in range(4):
                            t0 = ts * 512
                            for fc in range(FC):
                                def f(v, ts=ts, t0=t0, fc=fc):
                                    wt(v, "mm")
                                    b = fc % 2
                                    v.tensor_tensor(out=dtile[:, :],
                                                    in0=xln[:, fc, t0:t0 + 512],
                                                    in1=xln[:, fc, 1 + t0:1 + t0 + 512],
                                                    op=ALU.subtract)
                                    fire(v, "vx")
                                VE(f)

                                def f(t, fc=fc, t0=t0):
                                    wt(t, "vx")
                                    t.matmul(pj[:, :],
                                             sxpw2p[96:128, fc * P:(fc + 1) * P],
                                             xl_a[96:128, t0:t0 + 512],
                                             start=True, stop=True,
                                             tile_position=(96, 0))
                                    t.last.then_inc(inc("mm"), 1)
                                TE(f)
                                mm_mu = cnt["mm"]

                                def f(v, fc=fc, t0=t0, mm_mu=mm_mu):
                                    wt(v, "mm", mm_mu)
                                    b = fc % 2
                                    v.scalar_tensor_tensor(
                                        out=xbs[b][:, :], in0=pj[:, :],
                                        scalar=sb_xbias[:, 3 * FC + fc:3 * FC + fc + 1],
                                        in1=dtile[:, :], op0=ALU.add, op1=ALU.mult)
                                    v.drain()
                                    v.tensor_tensor(out=xbs[b][:, :], in0=xbs[b][:, :],
                                                    in1=xln[:, fc, 1 + t0:1 + t0 + 512],
                                                    op=ALU.add)
                                    fire(v, "vx")
                                VE(f)
                                vx_b = cnt["vx"]

                                def f(t, fc=fc, vx_b=vx_b, dw_v=dw_v):
                                    wt(t, "vx", vx_b)
                                    if fc == 0:
                                        t.wait_ge(sems["dw"], dw_v)
                                    b = fc % 2
                                    for hc in range(4):
                                        t.matmul(prs[hc][:, :],
                                                 wstr[:, fc, hc * P:(hc + 1) * P],
                                                 xbs[b][:, :],
                                                 start=(fc == 0), stop=(fc == FC - 1))
                                    t.last.then_inc(inc("mm"), 1)
                                TE(f)

                            mm_pr = cnt["mm"]
                            for hc in range(4):
                                def f(s, hc=hc, mm_pr=mm_pr):
                                    wt(s, "mm", mm_pr)
                                    wt(s, "dw")
                                    s.activation(out=vtile[:, :], in_=prs[hc][:, :],
                                                 func=ACT.Copy)
                                    fire(s, "sx")
                                SE(f)
                                sx_v = cnt["sx"]

                                def f(g, vp=vp, ts=ts, t0=t0, hc=hc, sx_v=sx_v):
                                    wt(g, "sx", sx_v)
                                    vcg = vp * 4 + hc
                                    dmago(g, "dw", v_spill.ap()[vcg, :, t0:t0 + 512],
                                          vtile[:, :])
                                GE(f)

                            def f(t):
                                wt(t, "sx")
                            TE(f)

                join_all()

                with ExitStack() as stk6:
                    qt = stk6.enter_context(nc.sbuf_tensor("bb_qt", [P, TCORE], BF16))
                    ktf = stk6.enter_context(nc.sbuf_tensor("bb_ktf", [P, 1024], BF16))
                    ktok = stk6.enter_context(nc.sbuf_tensor("bb_ktok", [P, 8, P], BF16))
                    vpre = stk6.enter_context(nc.sbuf_tensor("bb_vpre", [P, 2, 1024], BF16))
                    vtok = stk6.enter_context(nc.sbuf_tensor("bb_vtok", [P, 8, DV], BF16))
                    mid = stk6.enter_context(nc.sbuf_tensor("bb_mid", [P, NCH], F32))
                    cwc = stk6.enter_context(nc.sbuf_tensor("bb_cwc", [P, NCH], F32))
                    logf = stk6.enter_context(nc.sbuf_tensor("bb_logf", [P, NCH], F32))
                    Rf = stk6.enter_context(nc.sbuf_tensor("bb_R", [P, NCH], F32))
                    dtk = stk6.enter_context(nc.sbuf_tensor("bb_dtk", [P, NG], F32))
                    A_f = stk6.enter_context(nc.sbuf_tensor("bb_A", [P, DV], F32))
                    A_f2 = stk6.enter_context(nc.sbuf_tensor("bb_A2", [P, DV], F32))
                    A_b = stk6.enter_context(nc.sbuf_tensor("bb_Ab", [P, DV], BF16))
                    STs = stk6.enter_context(nc.sbuf_tensor("bb_ST", [P, P], BF16))
                    Dd = stk6.enter_context(nc.sbuf_tensor("bb_Dd", [P, P], BF16))
                    Pv = stk6.enter_context(nc.sbuf_tensor("bb_P", [P, 1], F32))
                    scr0 = stk6.enter_context(nc.sbuf_tensor("bb_s0", [P, 1024], F32))
                    scr1 = stk6.enter_context(nc.sbuf_tensor("bb_s1", [P, 1024], F32))
                    scr2 = stk6.enter_context(nc.sbuf_tensor("bb_s2", [P, 1024], F32))
                    pw = stk6.enter_context(nc.psum_tensor("bb_pw", [P, 1024], F32))
                    pt = stk6.enter_context(nc.psum_tensor("bb_pt", [P, 512], F32))
                    ptb = stk6.enter_context(nc.psum_tensor("bb_ptb", [P, 1024], BF16))
                    pST = stk6.enter_context(nc.psum_tensor("bb_pST", [P, 512], F32))
                    pO = stk6.enter_context(nc.psum_tensor("bb_pO", [P, 512], F32))
                    pU = stk6.enter_context(nc.psum_tensor("bb_pU", [P, 512], F32))
                    pd = stk6.enter_context(nc.psum_tensor("bb_pd", [P, 512], F32))

                    cc_done = {}

                    def emit_half(h, half):
                        hb = half * 1024
                        def f(t, h=h, hb=hb):
                            wt(t, "vx")
                            wt(t, "sx")
                            for q in range(2):
                                t.matmul(pw[:, q * 512:(q + 1) * 512],
                                         swB[:, h * P:(h + 1) * P],
                                         t2w[:, hb + q * 512: hb + (q + 1) * 512],
                                         start=True, stop=True)
                            t.last.then_inc(inc("mm"), 1)
                        TE(f)

                        def f(s, h=h):
                            wt(s, "mm")
                            s.activation(out=scr0[:, :], in_=pw[:, :], func=ACT.Exp,
                                         bias=sb_wb[:, h:h + 1])
                            fire(s, "sx")
                        SE(f)

                        def f(sy, h=h, hb=hb):
                            wt(sy, "mm")
                            dmago(sy, "dx",
                                  vpre.ap()[:, :, :],
                                  v_spill.ap()[2 * h:2 * h + 2, :, hb:hb + 1024]
                                  .rearrange("a p b -> p a b"))
                        SY(f)
                        dx_v = cnt["dx"]

                        for gp in range(4):
                            g0 = gp * 2
                            def f(t, g0=g0, gp=gp):
                                if gp == 0:
                                    wt(t, "sx")
                                wt(t, "vx")
                                for j in range(2):
                                    t.transpose(pt[:, j * P:(j + 1) * P],
                                                scr0[:, (g0 + j) * P:(g0 + j + 1) * P],
                                                sb_idf[:, :])
                                t.last.then_inc(inc("mm"), 1)
                            TE(f)

                            def f(v, g0=g0):
                                wt(v, "mm")
                                for j in range(2):
                                    v.tensor_copy(scr1[:, (g0 + j) * P:(g0 + j + 1) * P],
                                                  pt[:, j * P:(j + 1) * P])
                                fire(v, "vx")
                            VE(f)

                            def f(t, g0=g0):
                                wt(t, "vx")
                                for j in range(2):
                                    t.matmul(pt[:, j * P:(j + 1) * P], sb_tri[:, :],
                                             scr1[:, (g0 + j) * P:(g0 + j + 1) * P],
                                             start=True, stop=True)
                                t.last.then_inc(inc("mm"), 1)
                            TE(f)

                            def f(v, g0=g0):
                                wt(v, "mm")
                                for j in range(2):
                                    v.tensor_copy(scr1[:, (g0 + j) * P:(g0 + j + 1) * P],
                                                  pt[:, j * P:(j + 1) * P])
                                fire(v, "vx")
                            VE(f)

                            def f(t, g0=g0):
                                wt(t, "vx")
                                for j in range(2):
                                    t.transpose(pt[:, j * P:(j + 1) * P],
                                                scr1[:, (g0 + j) * P:(g0 + j + 1) * P],
                                                sb_idf[:, :])
                                t.last.then_inc(inc("mm"), 1)
                            TE(f)

                            def f(v, g0=g0):
                                wt(v, "mm")
                                for j in range(2):
                                    v.tensor_copy(scr2[:, (g0 + j) * P:(g0 + j + 1) * P],
                                                  pt[:, j * P:(j + 1) * P])
                                fire(v, "vx")
                            VE(f)

                        def f(t):
                            wt(t, "vx")
                        TE(f)


                        def f(v, half=half):
                            wt(v, "vx")
                            cw3 = scr2.ap().rearrange("p (c t) -> p c t", t=CH)
                            v.tensor_copy(mid.ap()[:, half * 16:half * 16 + 16]
                                          .unsqueeze(2), cw3[:, :, 31:32])
                            v.tensor_copy(cwc.ap()[:, half * 16:half * 16 + 16]
                                          .unsqueeze(2), cw3[:, :, 63:64])
                            v.drain()
                            midv = mid.ap()[:, half * 16:half * 16 + 16] \
                                .unsqueeze(2).to_broadcast([P, 16, CH])
                            s1v = scr1.ap().rearrange("p (c t) -> p c t", t=CH)
                            s0v = scr0.ap().rearrange("p (c t) -> p c t", t=CH)
                            s2v = scr2.ap().rearrange("p (c t) -> p c t", t=CH)
                            v.tensor_tensor(out=scr1[:, :], in0=scr2[:, :],
                                            in1=scr0[:, :], op=ALU.add)
                            v.drain()
                            v.tensor_tensor(out=s1v, in0=s1v, in1=midv,
                                            op=ALU.subtract)
                            v.drain()
                            v.tensor_tensor(out=s0v, in0=midv, in1=s2v,
                                            op=ALU.subtract)
                            fire(v, "vx")
                        VE(f)

                        def f(s):
                            wt(s, "vx")
                            s.activation(out=scr1[:, :], in_=scr1[:, :], func=ACT.Exp)
                            s.drain()
                            s.activation(out=scr0[:, :], in_=scr0[:, :], func=ACT.Exp)
                            fire(s, "sx")
                        SE(f)

                        def f(v, h=h, hb=hb):
                            wt(v, "sx")
                            v.tensor_tensor(out=qt[:, hb:hb + 1024],
                                            in0=r_sb[:, h, hb:hb + 1024],
                                            in1=scr1[:, :], op=ALU.mult)
                            v.drain()
                            v.tensor_tensor(out=ktf[:, :],
                                            in0=k_sb[:, h, hb:hb + 1024],
                                            in1=scr0[:, :], op=ALU.mult)
                            fire(v, "vx")
                        VE(f)

                        vx_tk = {}
                        for gg in range(8):
                            def f(t, gg=gg, dx_v=dx_v):
                                if gg == 0:
                                    wt(t, "vx")
                                    t.wait_ge(sems["dx"], dx_v)
                                if gg >= 1:
                                    wt(t, "vx", vx_tk[gg - 1])
                                t.transpose(ptb[:, 0:P], ktf[:, gg * P:(gg + 1) * P],
                                            sb_idb[:, :])
                                t.transpose(ptb[:, P:2 * P],
                                            vpre[:, 0, gg * P:(gg + 1) * P],
                                            sb_idb[:, :])
                                t.transpose(ptb[:, 2 * P:3 * P],
                                            vpre[:, 1, gg * P:(gg + 1) * P],
                                            sb_idb[:, :])
                                t.last.then_inc(inc("mm"), 1)
                            TE(f)
                            mm_tk = cnt["mm"]

                            def f(v, gg=gg, mm_tk=mm_tk):
                                wt(v, "mm", mm_tk)
                                v.tensor_copy(ktok[:, gg, :], ptb[:, 0:P])
                                v.tensor_copy(vtok[:, gg, 0:P], ptb[:, P:2 * P])
                                v.tensor_copy(vtok[:, gg, P:2 * P], ptb[:, 2 * P:3 * P])
                                fire(v, "vx")
                                vx_tk[gg] = cnt["vx"]
                            VE(f)

                        def f(v, h=h, hb=hb):
                            wt(v, "vx")
                            v.tensor_scalar(out=scr0[:, :], in0=r_sb[:, h, hb:hb + 1024],
                                            scalar1=sb_u[:, h:h + 1], scalar2=None,
                                            op0=ALU.mult)
                            v.drain()
                            v.tensor_tensor(out=scr0[:, :], in0=scr0[:, :],
                                            in1=k_sb[:, h, hb:hb + 1024], op=ALU.mult)
                            fire(v, "vx")
                        VE(f)

                        def f(t, half=half):
                            wt(t, "vx")
                            for gg in range(8):
                                t.matmul(pd[:, half * 8 + gg:half * 8 + gg + 1],
                                         scr0[:, gg * P:(gg + 1) * P],
                                         sb_onescol[:, :], start=True, stop=True)
                            t.last.then_inc(inc("mm"), 1)
                        TE(f)

                        def f(v, half=half):
                            wt(v, "mm")
                            v.tensor_copy(dtk[:, half * 8:half * 8 + 8],
                                          pd[:, half * 8:half * 8 + 8])
                            fire(v, "vx")
                        VE(f)

                        for gg in range(8):
                            g = half * 8 + gg

                            def f(t, gg=gg):
                                wt(t, "vx")
                                wt(t, "sx")
                                t.matmul(pST[:, 0:P], ktf[:, gg * P:(gg + 1) * P],
                                         qt[:, (half * 8 + gg) * P:
                                            (half * 8 + gg + 1) * P],
                                         start=True, stop=True)
                                t.last.then_inc(inc("mm"), 1)
                            TE(f)
                            mm_st = cnt["mm"]

                            def f(v, g=g, mm_st=mm_st):
                                wt(v, "mm", mm_st)
                                v.select(out=STs[:, :], mask=sb_mask[:, :],
                                         on_true=pST[:, 0:P], on_false=sb_zero[:, :],
                                         add_drain=True)
                                v.drain()
                                v.tensor_scalar(out=Dd[:, :], in0=sb_idb[:, :],
                                                scalar1=dtk[:, g:g + 1], scalar2=None,
                                                op0=ALU.mult)
                                v.drain()
                                v.tensor_tensor(out=STs[:, :], in0=STs[:, :],
                                                in1=Dd[:, :], op=ALU.add)
                                fire(v, "vx")
                            VE(f)
                            vx_st = cnt["vx"]

                            def f(t, g=g, gg=gg, vx_st=vx_st):
                                wt(t, "vx", vx_st)
                                wt(t, "sx")
                                t.matmul(pO[:, 0:DV], STs[:, :], vtok[:, gg, :],
                                         start=True, stop=False, skip_group_check=True)
                                t.matmul(pO[0:CH, 0:DV], qt[:, g * P:g * P + CH],
                                         A_b[:, :], start=False, stop=False,
                                         skip_group_check=True)
                                t.matmul(pU[:, 0:DV], ktok[0:CH, gg, :],
                                         vtok[0:CH, gg, :], start=True, stop=True,
                                         skip_group_check=True)
                                t.last.then_inc(inc("mm"), 1)
                            TE(f)
                            mm_u0 = cnt["mm"]

                            def f(v, mm_u0=mm_u0):
                                wt(v, "mm", mm_u0)
                                v.scalar_tensor_tensor(out=A_f2[:, :], in0=A_f[:, :],
                                                       scalar=1.0, in1=pU[:, 0:DV],
                                                       op0=ALU.mult, op1=ALU.add)
                                fire(v, "vx")
                            VE(f)

                            def f(v, g=g):
                                v.drain()
                                v.tensor_scalar(out=A_f[:, :], in0=A_f2[:, :],
                                                scalar1=logf[:, 2 * g:2 * g + 1],
                                                scalar2=None, op0=ALU.mult)
                                v.drain()
                                v.tensor_copy(A_b[:, :], A_f[:, :])
                                fire(v, "vx")
                            VE(f)
                            sx_a0 = cnt["vx"]

                            def f(t, g=g, gg=gg, sx_a0=sx_a0):
                                wt(t, "vx", sx_a0)
                                t.matmul(pO[CH:P, 0:DV],
                                         qt[:, g * P + CH:(g + 1) * P], A_b[:, :],
                                         start=False, stop=True, skip_group_check=True)
                                t.matmul(pU[:, 0:DV], ktok[CH:P, gg, :],
                                         vtok[CH:P, gg, :], start=True, stop=True,
                                         skip_group_check=True)
                                t.last.then_inc(inc("mm"), 1)
                            TE(f)
                            mm_u1 = cnt["mm"]

                            def f(v, g=g, h=h, mm_u1=mm_u1):
                                wt(v, "mm", mm_u1)
                                v.scalar_tensor_tensor(out=A_f2[:, :], in0=A_f[:, :],
                                                       scalar=1.0, in1=pU[:, 0:DV],
                                                       op0=ALU.mult, op1=ALU.add)
                                v.drain()
                                v.tensor_copy(o_sb[:, g, h * DV:(h + 1) * DV], pO[:, 0:DV])
                                fire(v, "vx")
                            VE(f)

                            def f(v, g=g):
                                v.drain()
                                v.tensor_scalar(out=A_f[:, :], in0=A_f2[:, :],
                                                scalar1=logf[:, 2 * g + 1:2 * g + 2],
                                                scalar2=None, op0=ALU.mult)
                                v.drain()
                                v.tensor_copy(A_b[:, :], A_f[:, :])
                                fire(v, "vx")
                            VE(f)

                    def emit_passA(h):
                        pass

                    for h in range(HEADS):
                        for half in range(2):
                            hb = half * 1024

                            def f(t, h=h, hb=hb):
                                wt(t, "vx")
                                wt(t, "sx")
                                for q in range(2):
                                    t.matmul(pw[:, q * 512:(q + 1) * 512],
                                             swB[:, h * P:(h + 1) * P],
                                             t2w[:, hb + q * 512: hb + (q + 1) * 512],
                                             start=True, stop=True)
                                t.last.then_inc(inc("mm"), 1)
                            TE(f)

                            def f(s, h=h):
                                wt(s, "mm")
                                s.activation(out=scr0[:, :], in_=pw[:, :], func=ACT.Exp,
                                             bias=sb_wb[:, h:h + 1])
                                fire(s, "sx")
                            SE(f)

                            for gp in range(4):
                                g0 = gp * 2
                                def f(t, g0=g0, gp=gp):
                                    if gp == 0:
                                        wt(t, "sx")
                                    wt(t, "vx")
                                    for j in range(2):
                                        t.transpose(pt[:, j * P:(j + 1) * P],
                                                    scr0[:, (g0 + j) * P:(g0 + j + 1) * P],
                                                    sb_idf[:, :])
                                    t.last.then_inc(inc("mm"), 1)
                                TE(f)

                                def f(v, g0=g0):
                                    wt(v, "mm")
                                    for j in range(2):
                                        v.tensor_copy(scr1[:, (g0 + j) * P:(g0 + j + 1) * P],
                                                      pt[:, j * P:(j + 1) * P])
                                    fire(v, "vx")
                                VE(f)

                                def f(t, g0=g0):
                                    wt(t, "vx")
                                    for j in range(2):
                                        t.matmul(pt[:, j * P:(j + 1) * P], sb_tri[:, :],
                                                 scr1[:, (g0 + j) * P:(g0 + j + 1) * P],
                                                 start=True, stop=True)
                                    t.last.then_inc(inc("mm"), 1)
                                TE(f)

                                def f(v, g0=g0):
                                    wt(v, "mm")
                                    for j in range(2):
                                        v.tensor_copy(scr1[:, (g0 + j) * P:(g0 + j + 1) * P],
                                                      pt[:, j * P:(j + 1) * P])
                                    fire(v, "vx")
                                VE(f)

                                def f(t, g0=g0):
                                    wt(t, "vx")
                                    for j in range(2):
                                        t.transpose(pt[:, j * P:(j + 1) * P],
                                                    scr1[:, (g0 + j) * P:(g0 + j + 1) * P],
                                                    sb_idf[:, :])
                                    t.last.then_inc(inc("mm"), 1)
                                TE(f)

                                def f(v, g0=g0):
                                    wt(v, "mm")
                                    for j in range(2):
                                        v.tensor_copy(scr2[:, (g0 + j) * P:(g0 + j + 1) * P],
                                                      pt[:, j * P:(j + 1) * P])
                                    fire(v, "vx")
                                VE(f)

                            def f(t):
                                wt(t, "vx")
                            TE(f)


                            def f(v, half=half):
                                wt(v, "vx")
                                cw3 = scr2.ap().rearrange("p (c t) -> p c t", t=CH)
                                v.tensor_copy(mid.ap()[:, half * 16:half * 16 + 16]
                                              .unsqueeze(2), cw3[:, :, 31:32])
                                v.tensor_copy(cwc.ap()[:, half * 16:half * 16 + 16]
                                              .unsqueeze(2), cw3[:, :, 63:64])
                                fire(v, "vx")
                            VE(f)

                        def f(v):
                            wt(v, "vx")
                            v.tensor_tensor(out=logf[:, :], in0=cwc[:, :], in1=mid[:, :],
                                            op=ALU.subtract)
                            v.drain()
                            v.tensor_tensor(out=logf[:, 0:NCH - 1],
                                            in0=logf[:, 0:NCH - 1],
                                            in1=mid[:, 1:NCH], op=ALU.add)
                            v.drain()
                            v.tensor_tensor_scan(out=scr0[:, 0:NCH], data0=logf[:, :],
                                                 data1=logf[:, :], initial=0.0,
                                                 op0=ALU.bypass, op1=ALU.add)
                            v.drain()
                            v.tensor_copy(Rf[:, 0:1], mid[:, 0:1])
                            v.drain()
                            v.tensor_tensor(out=Rf[:, 1:NCH], in0=scr0[:, 0:NCH - 1],
                                            in1=mid[:, 0:1].to_broadcast([P, NCH - 1]),
                                            op=ALU.add)
                            v.drain()
                            v.memset(A_f[:, :], 0.0)
                            v.memset(A_b[:, :], 0.0)
                            fire(v, "vx")
                        VE(f)

                        def f(s):
                            wt(s, "vx")
                            s.activation(out=logf[:, :], in_=logf[:, :], func=ACT.Exp)
                            s.drain()
                            s.activation(out=Rf[:, :], in_=Rf[:, :], func=ACT.Exp)
                            fire(s, "sx")
                        SE(f)

                        for half in range(2):
                            emit_half(h, half)

                        def f(v):
                            wt(v, "sx")
                            v.tensor_reduce(out=Pv[:, :], in_=cwc[:, :], axis=AX,
                                            op=ALU.add)
                            fire(v, "vx")
                        VE(f)

                        def f(s):
                            wt(s, "vx")
                            s.activation(out=Pv[:, :], in_=Pv[:, :], func=ACT.Exp)
                            fire(s, "sx")
                        SE(f)

                        def f(g, h=h):
                            wt(g, "sx")
                            dmago(g, "dw", cc_in.ap()[h, :, 0:DV], A_f[:, :])
                            g.dma_start(out=cc_in.ap()[h, :, DV:DV + 1], in_=Pv[:, :],
                                        allow_slow_non_contiguous=True) \
                                .then_inc(inc("dw", 16), 16)
                        GE(f)

                        def f(g, h=h):
                            wt(g, "dw")
                            g.collective_compute(
                                "AllGather", ALU.bypass,
                                replica_groups=[[0, 1, 2, 3], [4, 5, 6, 7]],
                                ins=[cc_in.ap()[h, :, :].opt()],
                                outs=[cc_out.ap()[h, :, :, :].opt()],
                            ).then_inc(inc("cc"), 1)
                        GE(f)
                        cc_done[h] = cnt["cc"]

                        cstk = ExitStack()
                        Gall = cstk.enter_context(nc.sbuf_tensor("bb_gal%d" % h, [P, 4, DV + 1], BF16))
                        Hin = cstk.enter_context(nc.sbuf_tensor("bb_hin%d" % h, [P, DV], F32))
                        Hcb = cstk.enter_context(nc.sbuf_tensor("bb_hcb%d" % h, [P, DV], BF16))
                        Ppr = cstk.enter_context(nc.sbuf_tensor("bb_pp%d" % h, [P, 4], F32))
                        def f(g, h=h):
                            g.wait_ge(sems["cc"], cc_done[h])
                            dmago(g, "dw", Gall.ap()[:, :, :],
                                  cc_out.ap()[h, :, :, :].rearrange("r p c -> p r c"))
                        GE(f)

                        def f(v):
                            wt(v, "dw")
                            for j in range(4):
                                v.tensor_scalar(out=Ppr[:, j:j + 1],
                                                in0=Gall[:, j, DV:DV + 1],
                                                scalar1=sb_rank[:, j:j + 1],
                                                scalar2=None, op0=ALU.mult)
                                v.drain()
                                v.tensor_tensor(out=Ppr[:, j:j + 1],
                                                in0=Ppr[:, j:j + 1],
                                                in1=sb_rank[:, 4 + j:5 + j], op=ALU.add)
                                v.drain()
                            v.tensor_scalar(out=Hin[:, :], in0=Gall[:, 0, 0:DV],
                                            scalar1=sb_rank[:, 0:1], scalar2=None,
                                            op0=ALU.mult)
                            v.drain()
                            for j in range(1, 4):
                                v.tensor_scalar(out=Hin[:, :], in0=Hin[:, :],
                                                scalar1=Ppr[:, j:j + 1], scalar2=None,
                                                op0=ALU.mult)
                                v.drain()
                                v.scalar_tensor_tensor(out=Hin[:, :],
                                                       in0=Gall[:, j, 0:DV],
                                                       scalar=sb_rank[:, j:j + 1],
                                                       in1=Hin[:, :],
                                                       op0=ALU.mult, op1=ALU.add)
                                v.drain()
                            fire(v, "vx")
                        VE(f)

                        for g in range(NG):
                            for c2 in range(2):
                                ch = 2 * g + c2

                                def f(v, ch=ch):
                                    wt(v, "mm")
                                    v.tensor_scalar(out=Hcb[:, :], in0=Hin[:, :],
                                                    scalar1=Rf[:, ch:ch + 1],
                                                    scalar2=None, op0=ALU.mult)
                                    fire(v, "vx")
                                VE(f)

                                def f(t, g=g, c2=c2):
                                    wt(t, "vx")
                                    t.matmul(pO[c2 * CH:(c2 + 1) * CH, 0:DV],
                                             qt[:, g * P + c2 * CH:
                                                g * P + (c2 + 1) * CH],
                                             Hcb[:, :],
                                             start=True, stop=True,
                                             skip_group_check=True)
                                    t.last.then_inc(inc("mm"), 1)
                                TE(f)

                            def f(v, g=g, h=h):
                                wt(v, "mm")
                                v.tensor_tensor(out=o_sb[:, g, h * DV:(h + 1) * DV],
                                                in0=o_sb[:, g, h * DV:(h + 1) * DV],
                                                in1=pO[:, 0:DV], op=ALU.add)
                                fire(v, "vx")
                            VE(f)

                            def f(t):
                                wt(t, "vx")
                            TE(f)

                        cstk.close()

                join_all()
                if DEBUG_ATTN:
                    def f(g):
                        dmago(g, "dw", dbg_qt[:, :], qt[:, :])
                        dmago(g, "dw", dbg_ktf[:, :], ktf[:, :])
                        dmago(g, "dw", dbg_A[:, :], A_f[:, :])
                        dmago(g, "dw", dbg_mid[:, :], mid[:, :])
                        dmago(g, "dw", dbg_cwc[:, :], cwc[:, :])
                        dmago(g, "dw", dbg_logf[:, :], logf[:, :])
                        dmago(g, "dw", dbg_dtk[:, :], dtk[:, :])
                        dmago(g, "dw", dbg_ktok[:, :],
                              ktok.ap().rearrange("p a b -> p (a b)"))
                        dmago(g, "dw", dbg_vtok[:, :],
                              vtok.ap().rearrange("p a b -> p (a b)"))
                        dmago(g, "dw", dbg_o[:, :],
                              o_sb.ap().rearrange("p a b -> p (a b)"))
                        dmago(g, "dw", dbg_r[:, :],
                              r_sb.ap().rearrange("p a b -> p (a b)"))
                        dmago(g, "dw", dbg_k[:, :],
                              k_sb.ap().rearrange("p a b -> p (a b)"))
                        dmago(g, "dw", dbg_t2w[:, :], t2w[:, :])
                        dmago(g, "dw", dbg_t2g[:, :], t2g[:, :])
                        dmago(g, "dw", dbg_v.rearrange("a p b -> (a p) b"),
                              v_spill.ap().rearrange("a p b -> (a p) b"))
                    GE(f)
                    join_all()
                ab_stk.close()

                with ExitStack() as stk7:
                    gsil = stk7.enter_context(nc.sbuf_tensor("cc_gs", [P, VD], BF16))
                    csq = stk7.enter_context(nc.sbuf_tensor("cc_sq", [P, VD], F32))
                    cst = stk7.enter_context(nc.sbuf_tensor("cc_st", [P, 6], F32))
                    ogf = stk7.enter_context(nc.sbuf_tensor("cc_og", [P, FC, TCORE], BF16))
                    wo = stk7.enter_context(nc.sbuf_tensor("cc_wo", [P, FC, D], BF16))
                    cgnw = stk7.enter_context(nc.sbuf_tensor("cc_gnw", [P, VD], F32))
                    cgnb = stk7.enter_context(nc.sbuf_tensor("cc_gnb", [P, VD], F32))
                    cgbr = stk7.enter_context(nc.sbuf_tensor("cc_gbr", [P, VD], F32))
                    sgB = stk7.enter_context(nc.sbuf_tensor("cc_gB", [64, VD], BF16))
                    pg0 = stk7.enter_context(nc.psum_tensor("cc_pg0", [P, 512], F32))
                    pg1 = stk7.enter_context(nc.psum_tensor("cc_pg1", [P, 512], F32))
                    cpt = stk7.enter_context(nc.psum_tensor("cc_pt", [P, 1024], BF16))
                    cpx = stk7.enter_context(nc.psum_tensor("cc_px", [P, 512], F32))

                    def f(sy):
                        dmago(sy, "dx", cgnw[:, :], gnw_rep[:, :])
                        dmago(sy, "dx", cgnb[:, :], gnb_rep[:, :])
                        dmago(sy, "dx", cgbr[:, :], gb_rep[:, :])
                    SY(f)

                    def f(g):
                        dmago(g, "dw", sgB[:, :], g_BT[:, :])
                        dmago(g, "dw", wo.ap()[:, :, :],
                              o_WT.rearrange("(a p) b -> p a b", p=P))
                    GE(f)

                    for g in range(NG):
                        def f(s, g=g):
                            if g == 0:
                                wt(s, "dx")
                                wt(s, "dw")
                            wt(s, "vx")
                            s.activation(out=csq[:, :], in_=o_sb[:, g, :],
                                         func=ACT.Square)
                            fire(s, "sx")
                        SE(f)

                        def f(v, g=g):
                            wt(v, "sx")
                            v.tensor_reduce(out=cst[:, 0:1], in_=o_sb[:, g, :],
                                            axis=AX, op=ALU.add)
                            v.tensor_reduce(out=cst[:, 1:2], in_=csq[:, :], axis=AX,
                                            op=ALU.add)
                            v.drain()
                            v.tensor_scalar(out=cst[:, 2:3], in0=cst[:, 0:1],
                                            scalar1=1.0 / VD, scalar2=None,
                                            op0=ALU.mult)
                            v.tensor_scalar(out=cst[:, 3:4], in0=cst[:, 1:2],
                                            scalar1=1.0 / VD, scalar2=None,
                                            op0=ALU.mult)
                            v.drain()
                            v.tensor_tensor(out=cst[:, 4:5], in0=cst[:, 2:3],
                                            in1=cst[:, 2:3], op=ALU.mult)
                            v.drain()
                            v.tensor_tensor(out=cst[:, 4:5], in0=cst[:, 3:4],
                                            in1=cst[:, 4:5], op=ALU.subtract)
                            fire(v, "vx")
                        VE(f)

                        def f(s, g=g):
                            wt(s, "vx")
                            s.activation(out=cst[:, 5:6], in_=cst[:, 4:5], func=ACT.Ln,
                                         bias=sb_eps[:, :])
                            s.drain()
                            s.activation(out=cst[:, 5:6], in_=cst[:, 5:6], func=ACT.Exp,
                                         scale=sb_nh[:, :])
                            fire(s, "sx")
                        SE(f)

                        def f(t, g=g):
                            wt(t, "sx")
                            t.matmul(pg0[:, :], t2g[:, g * P:(g + 1) * P],
                                     sgB[:, 0:512], start=True, stop=True)
                            t.matmul(pg1[:, :], t2g[:, g * P:(g + 1) * P],
                                     sgB[:, 512:1024], start=True, stop=True)
                            t.last.then_inc(inc("mm"), 1)
                        TE(f)

                        def f(v, g=g):
                            wt(v, "mm")
                            v.tensor_tensor(out=csq[:, 0:512], in0=pg0[:, :],
                                            in1=cgbr[:, 0:512], op=ALU.add)
                            v.tensor_tensor(out=csq[:, 512:1024], in0=pg1[:, :],
                                            in1=cgbr[:, 512:1024], op=ALU.add)
                            fire(v, "vx")
                        VE(f)

                        def f(s, g=g):
                            wt(s, "vx")
                            s.activation(out=gsil[:, :], in_=csq[:, :], func=ACT.Silu)
                            fire(s, "sx")
                        SE(f)

                        def f(v, g=g):
                            wt(v, "sx")
                            v.tensor_scalar(out=o_sb[:, g, :], in0=o_sb[:, g, :],
                                            scalar1=cst[:, 2:3], scalar2=cst[:, 5:6],
                                            op0=ALU.subtract, op1=ALU.mult)
                            v.drain()
                            v.tensor_tensor(out=o_sb[:, g, :], in0=o_sb[:, g, :],
                                            in1=cgnw[:, :], op=ALU.mult)
                            v.drain()
                            v.tensor_tensor(out=o_sb[:, g, :], in0=o_sb[:, g, :],
                                            in1=cgnb[:, :], op=ALU.add)
                            v.drain()
                            v.tensor_tensor(out=o_sb[:, g, :], in0=o_sb[:, g, :],
                                            in1=gsil[:, :], op=ALU.mult)
                            fire(v, "vx")
                        VE(f)

                        for vc in range(FC):
                            def f(t, g=g, vc=vc):
                                wt(t, "vx")
                                t.transpose(cpt[:, 0:P], o_sb[:, g, vc * P:(vc + 1) * P],
                                            sb_idb[:, :])
                                t.last.then_inc(inc("mm"), 1)
                            TE(f)

                            def f(v, g=g, vc=vc):
                                wt(v, "mm")
                                v.tensor_copy(ogf[:, vc, g * P:(g + 1) * P], cpt[:, 0:P])
                                fire(v, "vx")
                            VE(f)

                            def f(t):
                                wt(t, "vx")
                            TE(f)

                    vx_ow = {}
                    for dc in range(FC):
                        for ts in range(4):
                            t0 = ts * 512
                            idx = dc * 4 + ts

                            def f(t, dc=dc, t0=t0, idx=idx):
                                if idx == 0:
                                    wt(t, "dw")
                                    wt(t, "vx")
                                else:
                                    wt(t, "vx", vx_ow[idx - 1])
                                for vc in range(FC):
                                    t.matmul(cpx[:, :], wo[:, vc, dc * P:(dc + 1) * P],
                                             ogf[:, vc, t0:t0 + 512],
                                             start=(vc == 0), stop=(vc == FC - 1))
                                t.last.then_inc(inc("mm"), 1)
                            TE(f)

                            def f(v, dc=dc, t0=t0, idx=idx):
                                wt(v, "mm")
                                v.tensor_tensor(out=xres[:, dc, 1 + t0:1 + t0 + 512],
                                                in0=xres[:, dc, 1 + t0:1 + t0 + 512],
                                                in1=cpx[:, :], op=ALU.add)
                                fire(v, "vx")
                                vx_ow[idx] = cnt["vx"]
                            VE(f)

            join_all()

        def emit_store():
            with ExitStack() as stk8:
                toko = stk8.enter_context(nc.sbuf_tensor("st_tok", [P, D], F32))
                pso = stk8.enter_context(nc.psum_tensor("st_ps", [P, 512], F32))
                dx_st = {}
                for tt in range(NG):
                    for fc in range(FC):
                        def f(t, tt=tt, fc=fc):
                            if fc == 0 and tt >= 1:
                                wt(t, "dx", dx_st[tt - 1])
                            elif fc > 0:
                                wt(t, "vx")
                            t.transpose(pso[:, 0:P],
                                        xres[:, fc, 1 + tt * P:1 + (tt + 1) * P],
                                        sb_idf[:, :])
                            t.last.then_inc(inc("mm"), 1)
                        TE(f)

                        def f(v, fc=fc):
                            wt(v, "mm")
                            v.tensor_copy(toko[:, fc * P:(fc + 1) * P], pso[:, 0:P])
                            fire(v, "vx")
                        VE(f)

                        def f(t):
                            wt(t, "vx")
                        TE(f)

                    def f(sy, tt=tt):
                        wt(sy, "vx")
                        dmago(sy, "dx", out_t[tt * P:(tt + 1) * P, :], toko[:, :])
                        dx_st[tt] = cnt["dx"]
                    SY(f)

                def f(sy):
                    wt(sy, "dx")
                SY(f)

        with nc.sbuf_tensor("xln1", [P, FC, W], BF16) as xln1:
            if "p1" in phases:
                emit_load_ln1(xln1)
            if "ffn1" in phases:
                emit_ffn(xln1, "f1", 0, W)
            if "ln2" in phases:
                emit_ln(xln1, 0, W, "A")
            if "attn" in phases:
                emit_attn(xln1)
        with nc.sbuf_tensor("xln2", [P, FC, W], BF16) as xln2:
            if "ln3" in phases:
                emit_ln(xln2, 1, W, "B")
            if "ffn2" in phases:
                emit_ffn(xln2, "f2", 1, W)
        if "store" in phases or not phases or True:
            pass
        if "nostore" not in phases:
            emit_store()

    return nc


_NC_CACHE = None


def kernel(**inputs):
    global _NC_CACHE
    xfull = np.asarray(inputs["L0_attn_hidden_states"], dtype=np.float32)
    B, T, _ = xfull.shape
    xf = np.ascontiguousarray(xfull.reshape(B * T, D))

    def t(a):
        return np.ascontiguousarray(np.asarray(a, dtype=np.float32).T)

    i = inputs
    w1 = np.asarray(i["L0_ffn_w1"], np.float32)
    w2 = np.asarray(i["L0_ffn_w2"], np.float32)
    w3 = np.asarray(i["L0_ffn_w3"], np.float32)
    common = {
        "gate_wT": t(i["L0_ffn_gate_w"]),
        "w1T": np.ascontiguousarray(w1.transpose(0, 2, 1)),
        "w3T": np.ascontiguousarray(w3.transpose(0, 2, 1)),
        "w2T": np.ascontiguousarray(w2.transpose(0, 2, 1)),
        "ws1T": t(i["L0_ffn_ws1"]),
        "ws3T": t(i["L0_ffn_ws3"]),
        "ws2T": t(i["L0_ffn_ws2"]),
        "xp_W1T": t(i["L0_attn_xp_W1"]),
        "r_WT": t(i["L0_attn_r_W"]),
        "k_WT": t(i["L0_attn_k_W"]),
        "v_WT": t(i["L0_attn_v_W"]),
        "w_AT": t(i["L0_attn_w_A"]),
        "w_BT": t(i["L0_attn_w_B"]),
        "g_AT": t(i["L0_attn_g_A"]),
        "g_BT": t(i["L0_attn_g_B"]),
        "o_WT": t(i["L0_attn_o_W"]),
    }
    xpW2T = t(i["L0_attn_xp_W2"])
    common["xp_W2p"] = np.ascontiguousarray(xpW2T[0:128, :])
    common["xp_W2g"] = np.ascontiguousarray(xpW2T[128:160, :])

    def fpack(vec, n):
        v = np.asarray(vec, np.float32).reshape(n, P)
        return np.ascontiguousarray(v.T)

    common["xp_mu_p"] = fpack(i["L0_attn_xp_mu"], FC)
    xb = np.asarray(i["L0_attn_x_bias"], np.float32)
    xbp = np.stack([fpack(xb[n], FC) for n in range(5)], 1)
    common["xbias_p"] = np.ascontiguousarray(xbp.reshape(P, 5 * FC))
    common["wb_p"] = fpack(i["L0_attn_w_b"], HEADS)
    common["bonus_p"] = np.ascontiguousarray(np.asarray(i["L0_attn_bonus"], np.float32).T)
    common["gnw_rep"] = np.ascontiguousarray(
        np.broadcast_to(np.asarray(i["L0_attn_gn_w"], np.float32)[None, :], (P, VD)))
    common["gnb_rep"] = np.ascontiguousarray(
        np.broadcast_to(np.asarray(i["L0_attn_gn_b"], np.float32)[None, :], (P, VD)))
    common["gb_rep"] = np.ascontiguousarray(
        np.broadcast_to(np.asarray(i["L0_attn_g_b"], np.float32)[None, :], (P, VD)))
    common["gbias_rep"] = np.ascontiguousarray(
        np.broadcast_to(np.asarray(i["L0_ffn_gate_bias"], np.float32)[None, :], (P, NE)))
    common["ident_f"] = np.eye(P, dtype=np.float32)

    sidx = np.arange(P)
    blk = sidx[:, None] // CH == sidx[None, :] // CH
    tri = -(blk & (sidx[:, None] <= sidx[None, :])).astype(np.float32)
    common["tri_neg"] = np.ascontiguousarray(tri)
    msk = (blk & (sidx[:, None] < sidx[None, :])).astype(np.uint32)
    common["mask_st"] = np.ascontiguousarray(msk)
    common["neg_big"] = np.full((P, NE), -1e30, np.float32)
    oh = np.zeros((NE, NE * P), np.float32)
    for e in range(NE):
        oh[e, e * P:(e + 1) * P] = 1.0
    common["onehotE"] = oh

    if _NC_CACHE is None:
        _NC_CACHE = build()
    nc = _NC_CACHE

    in_maps = []
    for c in range(8):
        s = c * TCORE
        halo = np.zeros((1, D), np.float32) if c % 4 == 0 else xf[s - 1:s]
        shard = np.ascontiguousarray(np.concatenate([halo, xf[s:s + TCORE]], 0))
        r = c % 4
        rm = np.zeros((P, 8), np.float32)
        rm[:, 0:4] = (np.arange(4)[None, :] < r)
        rm[:, 4:8] = 1.0 - rm[:, 0:4]
        m = dict(common)
        m["x"] = shard
        m["rankm"] = np.ascontiguousarray(rm)
        in_maps.append(m)

    res = run_bass_kernel_spmd(nc, in_maps, core_ids=list(range(8)))
    out = np.concatenate([np.asarray(res.results[c]["out"]) for c in range(8)], 0)
    return out.reshape(B, T, D).astype(np.float32)
